# version 08
# baseline (speedup 1.0000x reference)
import contextlib
import numpy as np
import ml_dtypes
import concourse.bass as bass
import concourse.mybir as mybir
from concourse.bass_utils import run_bass_kernel_spmd

F32 = mybir.dt.float32
BF16 = mybir.dt.bfloat16
AF = mybir.ActivationFunctionType
ALU = mybir.AluOpType
AX = mybir.AxisListType

T = 2048
D = 2048
NT = T // 128
ND = D // 128
EPS = 1e-5
NCORES = 8


class Ct:

    def __init__(self, handle):
        self.h = handle
        self.n = 0

    def inc(self, k=1):
        self.n += k
        return self.n


class Prog:

    ENGS = ("t", "v", "s", "g", "y")

    def __init__(self, nc):
        self.nc = nc
        self.qs = {k: [] for k in self.ENGS}

    def t(self, f):
        self.qs["t"].append(f)

    def v(self, f):
        self.qs["v"].append(f)

    def s(self, f):
        self.qs["s"].append(f)

    def g(self, f):
        self.qs["g"].append(f)

    def y(self, f):
        self.qs["y"].append(f)

    def emit(self):
        def runner(fs):
            def go(e):
                for f in fs:
                    f(e)
            return go
        with self.nc.Block() as blk:
            if self.qs["t"]:
                blk.tensor(runner(self.qs["t"]))
            if self.qs["v"]:
                blk.vector(runner(self.qs["v"]))
            if self.qs["s"]:
                blk.scalar(runner(self.qs["s"]))
            if self.qs["g"]:
                blk.gpsimd(runner(self.qs["g"]))
            if self.qs["y"]:
                blk.sync(runner(self.qs["y"]))
        self.qs = {k: [] for k in self.ENGS}


SEM_KEYS = [
    "t", "v", "s",
    "qmi",
    "qx0", "qx1",
    "qst0", "qst1",
    "qcr",
    "qcs",
    "gfc0", "gfc1", "gfc2", "gfc3",
    "gms",
    "gkv",
    "gwq",
    "gwo",
    "gmw",
    "gwg0", "gwg1",
    "gwd0", "gwd1", "gwd2", "gwd3",
]


def mk_sems(nc, ctx):
    return {k: Ct(ctx.enter_context(nc.semaphore("sem_" + k)))
            for k in SEM_KEYS}



def emit_ln_finalize(P, S, s1, s2, rstd, nbias, tmp, n_elem, eps, ncols,
                     wait):
    inv = 1.0 / n_elem
    v, sc = S['v'], S['s']
    m = tmp[:, 0:ncols]
    ex2 = tmp[:, ncols:2 * ncols]
    mm = tmp[:, 2 * ncols:3 * ncols]

    def v1(e, wait=tuple(wait)):
        for h, val in wait:
            e.wait_ge(h, val)
        e.tensor_scalar_mul(m, s1[:, 0:ncols], inv)
        e.tensor_scalar_mul(ex2, s2[:, 0:ncols], inv).then_inc(v.h, 1)
    P.v(v1)
    w1 = v.inc()

    def v2(e, w1=w1):
        e.wait_ge(v.h, w1)
        e.tensor_mul(mm, m, m).then_inc(v.h, 1)
    P.v(v2)
    w2 = v.inc()

    def v3(e, w2=w2):
        e.wait_ge(v.h, w2)
        e.tensor_sub(ex2, ex2, mm).then_inc(v.h, 1)
    P.v(v3)
    w3 = v.inc()

    def v3b(e, w3=w3, eps=float(eps)):
        e.wait_ge(v.h, w3)
        e.tensor_scalar_add(ex2, ex2, eps).then_inc(v.h, 1)
    P.v(v3b)
    w3b = v.inc()

    def sf(e, w3b=w3b):
        e.wait_ge(v.h, w3b)
        e.activation(rstd[:, 0:ncols], ex2, AF.Sqrt, bias=0.0,
                     scale=1.0).then_inc(sc.h, 1)
    P.s(sf)
    ws = sc.inc()

    def v4(e, ws=ws):
        e.wait_ge(sc.h, ws)
        e.reciprocal(rstd[:, 0:ncols], rstd[:, 0:ncols]).then_inc(v.h, 1)
    P.v(v4)
    w4 = v.inc()

    def v5(e, w4=w4):
        e.wait_ge(v.h, w4)
        e.scalar_tensor_tensor(nbias[:, 0:ncols], m, -1.0, rstd[:, 0:ncols],
                               op0=ALU.mult, op1=ALU.mult).then_inc(v.h, 1)
    P.v(v5)
    return v.inc()


def emit_stats_tile(P, S, src_ap, k, s1, s2, trash, dma_wait):
    v, sc = S['v'], S['s']

    def st_v(e, src_ap=src_ap, k=k, dw=dma_wait):
        if dw is not None:
            e.wait_ge(dw[0], dw[1])
        e.reduce_sum(s1[:, k:k + 1], src_ap, axis=AX.X).then_inc(v.h, 1)
    P.v(st_v)
    vv = v.inc()

    def st_s(e, src_ap=src_ap, k=k, dw=dma_wait):
        if dw is not None:
            e.wait_ge(dw[0], dw[1])
        e.activation(trash, src_ap, AF.Square,
                     accum_out=s2[:, k:k + 1]).then_inc(sc.h, 1)
    P.s(st_s)
    sv = sc.inc()
    return vv, sv


def emit_factor_group(P, S, factors_in, fbuf, fscr, dt, fstate):
    v = S['v']
    gf = S['gfc0']
    acc = fbuf[:, 2 * D:3 * D]

    def ld0(e, dt=dt, pc=fstate.get('cast'), gh=gf.h):
        if pc is not None:
            e.wait_ge(gh, pc)
        e.dma_start(out=acc, in_=factors_in[0, dt * 128:(dt + 1) * 128, :]
                    ).then_inc(gh, 16)
    P.g(ld0)
    acc_ld = gf.inc(16)

    add_vals = []
    for r in range(1, 8):
        sl = (r - 1) % 2

        def ldr(e, dt=dt, r=r, sl=sl, av=list(add_vals), gh=gf.h):
            if r >= 3:
                e.wait_ge(v.h, av[r - 3])
            e.dma_start(out=fbuf[:, sl * D:(sl + 1) * D],
                        in_=factors_in[r, dt * 128:(dt + 1) * 128, :]
                        ).then_inc(gh, 16)
        P.g(ldr)
        ldv = gf.inc(16)

        def add(e, r=r, sl=sl, ldv=ldv, al=acc_ld, av=list(add_vals),
                gh=gf.h):
            e.wait_ge(gh, ldv)
            if r == 1:
                e.wait_ge(gh, al)
            else:
                e.wait_ge(v.h, av[-1])
            e.tensor_add(acc, acc, fbuf[:, sl * D:(sl + 1) * D]
                         ).then_inc(v.h, 1)
        P.v(add)
        add_vals.append(v.inc())

    def cast(e, dt=dt, av=add_vals[-1], gh=gf.h):
        e.wait_ge(v.h, av)
        e.dma_start(out=fscr[dt * 128:(dt + 1) * 128, :], in_=acc
                    ).then_inc(gh, 16)
    P.g(cast)
    fstate['cast'] = gf.inc(16)




def build_phase_a(nc, ctx, P, S, x_in, x1_out, ident_in, c_scratch,
                  dbg=None, factor_cb=None):
    t, v, sc = S['t'], S['v'], S['s']
    qmi = S['qmi']
    qx = [S['qx0'], S['qx1']]
    qst = [S['qst0'], S['qst1']]
    qcr, qcs = S['qcr'], S['qcs']

    id_sb = ctx.enter_context(nc.sbuf_tensor("id_sb", [128, 128], BF16))
    xbuf = ctx.enter_context(nc.sbuf_tensor("xbuf", [128, 2 * D], F32))
    trash = ctx.enter_context(nc.sbuf_tensor("trash", [128, D], BF16))
    s1 = ctx.enter_context(nc.sbuf_tensor("s1", [128, NT], F32))
    s2 = ctx.enter_context(nc.sbuf_tensor("s2", [128, NT], F32))
    rstd = ctx.enter_context(nc.sbuf_tensor("rstd", [128, NT], F32))
    nbias = ctx.enter_context(nc.sbuf_tensor("nbias", [128, NT], F32))
    tmp = ctx.enter_context(nc.sbuf_tensor("tmp", [128, 3 * NT], F32))
    xn = ctx.enter_context(nc.sbuf_tensor("xn", [128, NT * D], BF16))
    xnt = ctx.enter_context(nc.sbuf_tensor("xnt", [128, ND * T], BF16))
    wc = ctx.enter_context(nc.sbuf_tensor("wc", [128, 2 * D], BF16))
    wct = ctx.enter_context(nc.sbuf_tensor("wct", [128, ND * 512], BF16))
    cch = ctx.enter_context(nc.sbuf_tensor("cch", [128, 4 * 512], BF16))
    smax = ctx.enter_context(nc.sbuf_tensor("smax", [128, 4], F32))
    srs = ctx.enter_context(nc.sbuf_tensor("srs", [128, 8], F32))

    ps_s = ctx.enter_context(nc.psum_tensor("ps_s", [128, 2048], F32))
    ps_t0 = ctx.enter_context(nc.psum_tensor("ps_t0", [128, 1024], BF16))
    ps_t1 = ctx.enter_context(nc.psum_tensor("ps_t1", [128, 1024], BF16))
    ps_tt = [ps_t0, ps_t1]
    ps_c0 = ctx.enter_context(nc.psum_tensor("ps_c0", [128, 512], F32))
    ps_c1 = ctx.enter_context(nc.psum_tensor("ps_c1", [128, 512], F32))
    ps_c = [ps_c0, ps_c1]

    def ld_id(e):
        e.dma_start(out=id_sb[:, :], in_=ident_in).then_inc(qmi.h, 16)
    P.y(ld_id)
    id_ready = qmi.inc(16)

    stat_v, stat_s = [], []
    for k in range(NT):
        j = k % 2

        def ld(e, k=k, j=j, sv=list(stat_v), ss=list(stat_s), qh=qx[j].h):
            if k >= 2:
                e.wait_ge(v.h, sv[k - 2])
                e.wait_ge(sc.h, ss[k - 2])
            e.dma_start(out=xbuf[:, j * D:(j + 1) * D],
                        in_=x_in[k * 128:(k + 1) * 128, :]).then_inc(qh, 16)
        P.y(ld)
        dw = (qx[j].h, qx[j].inc(16))
        vv, sv_ = emit_stats_tile(P, S, xbuf[:, j * D:(j + 1) * D], k,
                                  s1, s2, trash[:, :], dw)
        stat_v.append(vv)
        stat_s.append(sv_)

    ln_ready = emit_ln_finalize(P, S, s1, s2, rstd, nbias, tmp, D, EPS, NT,
                                wait=[(v.h, stat_v[-1]), (sc.h, stat_s[-1])])

    norm_s = []
    tpos_v = []
    for k in range(NT):
        j = k % 2

        def ld(e, k=k, j=j, ns=list(norm_s), qh=qx[j].h,
               pv=stat_v[-1], ps=stat_s[-1]):
            if k >= 2:
                e.wait_ge(sc.h, ns[k - 2])
            else:
                e.wait_ge(v.h, pv)
                e.wait_ge(sc.h, ps)
            e.dma_start(out=xbuf[:, j * D:(j + 1) * D],
                        in_=x_in[k * 128:(k + 1) * 128, :]).then_inc(qh, 16)
        P.y(ld)
        dw = (qx[j].h, qx[j].inc(16))

        def nrm(e, k=k, j=j, dw=dw, lnr=ln_ready):
            e.wait_ge(dw[0], dw[1])
            e.wait_ge(v.h, lnr)
            e.activation(xn[:, k * D:(k + 1) * D], xbuf[:, j * D:(j + 1) * D],
                         AF.Identity, bias=nbias[:, k:k + 1],
                         scale=rstd[:, k:k + 1]).then_inc(sc.h, 1)
        P.s(nrm)
        norm_s.append(sc.inc())

        for grp in range(2):
            pst = ps_tt[grp]

            def tp(e, k=k, grp=grp, pst=pst, nsv=norm_s[-1], idr=id_ready,
                   tv=list(tpos_v)):
                e.wait_ge(sc.h, nsv)
                if k == 0 and grp == 0:
                    e.wait_ge(qmi.h, idr)
                if len(tv) >= 2:
                    e.wait_ge(v.h, tv[-2])
                for jj in range(8):
                    jt = grp * 8 + jj
                    ins = e.transpose(
                        pst[:, jj * 128:(jj + 1) * 128],
                        xn[:, k * D + jt * 128:k * D + (jt + 1) * 128],
                        id_sb[:, :])
                    if jj == 7:
                        ins.then_inc(t.h, 1)
            P.t(tp)
            tval = t.inc()

            def ev(e, k=k, grp=grp, pst=pst, tval=tval):
                e.wait_ge(t.h, tval)
                for jj in range(8):
                    jt = grp * 8 + jj
                    ins = e.tensor_copy(
                        xnt[:, jt * T + k * 128:jt * T + (k + 1) * 128],
                        pst[:, jj * 128:(jj + 1) * 128])
                    if jj == 7:
                        ins.then_inc(v.h, 1)
            P.v(ev)
            tpos_v.append(v.inc())
        if factor_cb is not None:
            factor_cb(k)

    xn_all_s = norm_s[-1]
    xnt_all_v = tpos_v[-1]

    if dbg is not None:
        xn_dbg, xnt_dbg = dbg

        def dump(e, xa=xn_all_s, xv=xnt_all_v, qh=S['qcr'].h):
            e.wait_ge(sc.h, xa)
            e.wait_ge(v.h, xv)
            e.dma_start(out=xn_dbg, in_=xn[:, :]).then_inc(qh, 16)
            e.dma_start(out=xnt_dbg, in_=xnt[:, :]).then_inc(qh, 16)
        P.y(dump)
        S['qcr'].inc(32)

    exp_done = []
    cst_done = []
    cev_v = []
    wt_v = []
    for ic in range(4):
        for il in range(4):
            it = ic * 4 + il
            wpar = il % 2

            def smm(e, it=it, xa=xn_all_s, ed=list(exp_done)):
                e.wait_ge(sc.h, xa)
                if ed:
                    e.wait_ge(sc.h, ed[-1])
                for jt in range(4):
                    for ct in range(NT):
                        ins = e.matmul(
                            ps_s[:, jt * 512:(jt + 1) * 512],
                            xn[:, ct * D + it * 128:ct * D + it * 128 + 128],
                            xn[:, ct * D + jt * 512:ct * D + (jt + 1) * 512],
                            start=(ct == 0), stop=(ct == NT - 1))
                        if jt == 3 and ct == NT - 1:
                            ins.then_inc(t.h, 1)
            P.t(smm)
            smm_t = t.inc()

            def mx(e, il=il, smm_t=smm_t):
                e.wait_ge(t.h, smm_t)
                e.reduce_max(smax[:, il:il + 1], ps_s[:, :], axis=AX.X,
                             negate=True).then_inc(v.h, 1)
            P.v(mx)
            mx_v = v.inc()

            def ex(e, il=il, wpar=wpar, mx_v=mx_v, wv=list(wt_v)):
                e.wait_ge(v.h, mx_v)
                if len(wv) >= 4:
                    e.wait_ge(v.h, wv[-4])
                e.activation(wc[:, wpar * D:(wpar + 1) * D], ps_s[:, :],
                             AF.Exp, bias=smax[:, il:il + 1], scale=1.0,
                             accum_out=srs[:, il:il + 1]).then_inc(sc.h, 1)
            P.s(ex)
            exp_done.append(sc.inc())

            def rcp(e, il=il, ev_=exp_done[-1]):
                e.wait_ge(sc.h, ev_)
                e.reciprocal(srs[:, 4 + il:5 + il],
                             srs[:, il:il + 1]).then_inc(v.h, 1)
            P.v(rcp)
            rcp_v = v.inc()

            def nrmw(e, wpar=wpar, il=il, rcp_v=rcp_v):
                e.wait_ge(v.h, rcp_v)
                e.tensor_scalar_mul(wc[:, wpar * D:(wpar + 1) * D],
                                    wc[:, wpar * D:(wpar + 1) * D],
                                    srs[:, 4 + il:5 + il]).then_inc(v.h, 1)
            P.v(nrmw)
            wn_v = v.inc()

            for grp in range(2):
                pst = ps_tt[grp]

                def tpw(e, il=il, wpar=wpar, grp=grp, pst=pst, wn=wn_v,
                        wv=list(wt_v), tv=xnt_all_v):
                    e.wait_ge(v.h, wn)
                    if len(wv) >= 2:
                        e.wait_ge(v.h, wv[-2])
                    else:
                        e.wait_ge(v.h, tv)
                    for jj in range(8):
                        jt = grp * 8 + jj
                        ins = e.transpose(
                            pst[:, jj * 128:(jj + 1) * 128],
                            wc[:, wpar * D + jt * 128:wpar * D + (jt + 1) * 128],
                            id_sb[:, :])
                        if jj == 7:
                            ins.then_inc(t.h, 1)
                P.t(tpw)
                tval = t.inc()

                def evw(e, il=il, grp=grp, pst=pst, tval=tval):
                    e.wait_ge(t.h, tval)
                    for jj in range(8):
                        jt = grp * 8 + jj
                        ins = e.tensor_copy(
                            wct[:, jt * 512 + il * 128:jt * 512 + (il + 1) * 128],
                            pst[:, jj * 128:(jj + 1) * 128])
                        if jj == 7:
                            ins.then_inc(v.h, 1)
                P.v(evw)
                wt_v.append(v.inc())
        wct_ready = wt_v[-1]

        for ct in range(NT):
            pc = ps_c[ct % 2]
            slot = ct % 4

            def cmm(e, ct=ct, pc=pc, wr=wct_ready, cv=list(cev_v)):
                e.wait_ge(v.h, wr)
                if len(cv) >= 2:
                    e.wait_ge(v.h, cv[-2])
                for jt in range(ND):
                    ins = e.matmul(
                        pc[:, :],
                        xnt[:, jt * T + ct * 128:jt * T + (ct + 1) * 128],
                        wct[:, jt * 512:(jt + 1) * 512],
                        start=(jt == 0), stop=(jt == ND - 1))
                    if jt == ND - 1:
                        ins.then_inc(t.h, 1)
            P.t(cmm)
            cval = t.inc()

            def cev(e, slot=slot, pc=pc, cval=cval, cs=list(cst_done),
                    qh=qcs.h):
                e.wait_ge(t.h, cval)
                if len(cs) >= 1 and slot == 0:
                    e.wait_ge(qh, cs[-1])
                e.tensor_copy(cch[:, slot * 512:(slot + 1) * 512],
                              pc[:, :]).then_inc(v.h, 1)
            P.v(cev)
            cev_v.append(v.inc())

            if slot == 3:
                def cst(e, ic=ic, ct=ct, cv=cev_v[-1]):
                    e.wait_ge(v.h, cv)
                    dst = c_scratch.rearrange("(ct p) i -> p ct i", p=128)
                    e.dma_start(
                        out=dst[:, ct - 3:ct + 1, ic * 512:(ic + 1) * 512],
                        in_=cch[:, :].rearrange("p (q i) -> p q i", i=512)
                        ).then_inc(qcs.h, 16)
                P.y(cst)
                cst_done.append(qcs.inc(16))

    crow = xn
    for k in range(NT):
        def ld(e, k=k, cs=cst_done[-1]):
            if k == 0:
                e.wait_ge(qcs.h, cs)
            e.dma_start(out=crow[:, k * D:(k + 1) * D],
                        in_=c_scratch[k * 128:(k + 1) * 128, :]
                        ).then_inc(qcr.h, 16)
        P.y(ld)
        qcr.inc(16)
    crow_all = (qcr.h, qcr.n)

    stat4_v, stat4_s = [], []
    for k in range(NT):
        vv, sv_ = emit_stats_tile(P, S, crow[:, k * D:(k + 1) * D], k,
                                  s1, s2, trash[:, :], crow_all)
        stat4_v.append(vv)
        stat4_s.append(sv_)

    ln4 = emit_ln_finalize(P, S, s1, s2, rstd, nbias, tmp, D, EPS, NT,
                           wait=[(v.h, stat4_v[-1]), (sc.h, stat4_s[-1])])

    st1_q = []
    add_v = []
    for k in range(NT):
        j = k % 2

        def ldx(e, k=k, j=j, sq=list(st1_q), qh=qx[j].h, sh=qst[j].h,
                ps=norm_s[-1]):
            if k >= 2:
                e.wait_ge(sh, sq[k - 2])
            else:
                e.wait_ge(sc.h, ps)
            e.dma_start(out=xbuf[:, j * D:(j + 1) * D],
                        in_=x_in[k * 128:(k + 1) * 128, :]).then_inc(qh, 16)
        P.y(ldx)
        dxw = (qx[j].h, qx[j].inc(16))

        def nrm4(e, k=k, ln4=ln4, av=list(add_v)):
            e.wait_ge(v.h, ln4)
            if av:
                e.wait_ge(v.h, av[-1])
            e.activation(trash[:, :], crow[:, k * D:(k + 1) * D], AF.Identity,
                         bias=nbias[:, k:k + 1],
                         scale=rstd[:, k:k + 1]).then_inc(sc.h, 1)
        P.s(nrm4)
        nv = sc.inc()

        def add4(e, k=k, j=j, nv=nv, dxw=dxw):
            e.wait_ge(sc.h, nv)
            e.wait_ge(dxw[0], dxw[1])
            e.tensor_tensor(xbuf[:, j * D:(j + 1) * D],
                            xbuf[:, j * D:(j + 1) * D], trash[:, :],
                            op=ALU.add).then_inc(v.h, 1)
        P.v(add4)
        add_v.append(v.inc())

        def stx(e, k=k, j=j, av=add_v[-1], sh=qst[j].h):
            e.wait_ge(v.h, av)
            e.dma_start(out=x1_out[k * 128:(k + 1) * 128, :],
                        in_=xbuf[:, j * D:(j + 1) * D]).then_inc(sh, 16)
        P.y(stx)
        st1_q.append(qst[j].inc(16))

    def fin(e, l0=qst[0].n, l1=qst[1].n):
        e.wait_ge(qst[0].h, l0)
        e.wait_ge(qst[1].h, l1)
    P.y(fin)



def _ident():
    return np.eye(128).astype(ml_dtypes.bfloat16)


def build_a_program():
    nc = bass.Bass("TRN2", target_bir_lowering=False)
    x_in = nc.dram_tensor("x", [T, D], F32, kind="ExternalInput").ap()
    ident_in = nc.dram_tensor("ident", [128, 128], BF16,
                              kind="ExternalInput").ap()
    x1_out = nc.dram_tensor("x1", [T, D], F32, kind="ExternalOutput").ap()
    c_scr = nc.dram_tensor("c_scr", [T, D], BF16, kind="Internal").ap()
    fscr = nc.dram_tensor("fscr", [D, D], BF16, kind="Internal").ap()
    with contextlib.ExitStack() as ctx:
        S = mk_sems(nc, ctx)
        P = Prog(nc)
        build_phase_a(nc, ctx, P, S, x_in, x1_out, ident_in, c_scr)
        P.emit()
    return nc


_PROGS = {}


def _get_programs():
    if "a" not in _PROGS:
        _PROGS["a"] = build_a_program()
        _PROGS["b"] = build_b_program()
        _PROGS["c"] = build_c_program()
        _PROGS["d"] = build_d_program()
    return _PROGS


def _tile_gateup(w):
    wt = w.T
    return np.ascontiguousarray(
        wt.reshape(16, 128, 44, 128).transpose(2, 1, 0, 3).reshape(44, 128, 2048))


FUSED = True


def _prep_host(L0_ffn_factors, L0_ffn_bias, L1_attn_norm_scale,
               L1_attn_split_scale, L1_attn_Wq, L1_attn_Wk, L1_attn_Wv,
               L1_attn_Wo, L1_ffn_gate_w, L1_ffn_up_w, L1_ffn_down_w,
               L1_ffn_gate_s, L1_ffn_up_s, L1_ffn_down_s):
    f32 = np.float32
    return {
        "factors": np.ascontiguousarray(np.asarray(L0_ffn_factors, f32)),
        "fbias": np.ascontiguousarray(np.asarray(L0_ffn_bias, f32)),
        "wqt": np.ascontiguousarray(np.asarray(L1_attn_Wq, f32).T),
        "wkt": np.ascontiguousarray(np.asarray(L1_attn_Wk, f32).T),
        "wvt": np.ascontiguousarray(np.asarray(L1_attn_Wv, f32).T),
        "wot": np.ascontiguousarray(np.asarray(L1_attn_Wo, f32).T),
        "nscol": np.ascontiguousarray(
            np.asarray(L1_attn_norm_scale, f32).reshape(16, 128).T),
        "sscol": np.ascontiguousarray(
            np.asarray(L1_attn_split_scale, f32).reshape(16, 128).T),
        "gate_t": _tile_gateup(np.asarray(L1_ffn_gate_w, f32)),
        "up_t": _tile_gateup(np.asarray(L1_ffn_up_w, f32)),
        "down_t": np.ascontiguousarray(np.asarray(L1_ffn_down_w, f32).T),
        "gs": np.ascontiguousarray(
            np.asarray(L1_ffn_gate_s, f32).reshape(44, 128).T),
        "us": np.ascontiguousarray(
            np.asarray(L1_ffn_up_s, f32).reshape(44, 128).T),
        "ds": np.ascontiguousarray(np.asarray(L1_ffn_down_s, f32)),
        "ident": _ident(),
    }


def kernel(L0_attn_x, L0_ffn_factors, L0_ffn_bias, L1_attn_norm_scale,
           L1_attn_split_scale, L1_attn_Wq, L1_attn_Wk, L1_attn_Wv,
           L1_attn_Wo, L1_ffn_gate_w, L1_ffn_up_w, L1_ffn_down_w,
           L1_ffn_gate_s, L1_ffn_up_s, L1_ffn_down_s):
    if FUSED:
        if "full" not in _PROGS:
            _PROGS["full"] = build_full_program()
        nc_full = _PROGS["full"]
        x = np.ascontiguousarray(np.asarray(L0_attn_x, dtype=np.float32))
        shared = _prep_host(L0_ffn_factors, L0_ffn_bias, L1_attn_norm_scale,
                            L1_attn_split_scale, L1_attn_Wq, L1_attn_Wk,
                            L1_attn_Wv, L1_attn_Wo, L1_ffn_gate_w,
                            L1_ffn_up_w, L1_ffn_down_w, L1_ffn_gate_s,
                            L1_ffn_up_s, L1_ffn_down_s)
        in_maps = [dict(shared, x=np.ascontiguousarray(x[b]))
                   for b in range(NCORES)]
        res = run_bass_kernel_spmd(nc_full, in_maps,
                                   core_ids=list(range(NCORES)))
        return np.stack([np.asarray(r["x4"]) for r in res.results]
                        ).astype(np.float32)
    progs = _get_programs()
    ident = _ident()
    cores = list(range(NCORES))
    x = np.ascontiguousarray(np.asarray(L0_attn_x, dtype=np.float32))

    in_maps = [{"x": np.ascontiguousarray(x[b]), "ident": ident}
               for b in range(NCORES)]
    res = run_bass_kernel_spmd(progs["a"], in_maps, core_ids=cores)
    x1 = [np.asarray(r["x1"]) for r in res.results]

    facs = np.ascontiguousarray(np.asarray(L0_ffn_factors, dtype=np.float32))
    fbias = np.ascontiguousarray(np.asarray(L0_ffn_bias, dtype=np.float32))
    in_maps = [{"x1": x1[b], "factors": facs, "fbias": fbias, "ident": ident}
               for b in range(NCORES)]
    res = run_bass_kernel_spmd(progs["b"], in_maps, core_ids=cores)
    x2 = [np.asarray(r["x2"]) for r in res.results]

    wqt = np.ascontiguousarray(np.asarray(L1_attn_Wq, dtype=np.float32).T)
    wkt = np.ascontiguousarray(np.asarray(L1_attn_Wk, dtype=np.float32).T)
    wvt = np.ascontiguousarray(np.asarray(L1_attn_Wv, dtype=np.float32).T)
    wot = np.ascontiguousarray(np.asarray(L1_attn_Wo, dtype=np.float32).T)
    nscol = np.ascontiguousarray(
        np.asarray(L1_attn_norm_scale, dtype=np.float32).reshape(16, 128).T)
    sscol = np.ascontiguousarray(
        np.asarray(L1_attn_split_scale, dtype=np.float32).reshape(16, 128).T)
    in_maps = [{"x2": x2[b], "wqt": wqt, "wkt": wkt, "wvt": wvt, "wot": wot,
                "nscol": nscol, "sscol": sscol, "ident": ident}
               for b in range(NCORES)]
    res = run_bass_kernel_spmd(progs["c"], in_maps, core_ids=cores)
    x3 = [np.asarray(r["x3"]) for r in res.results]

    gate_t = _tile_gateup(np.asarray(L1_ffn_gate_w, dtype=np.float32))
    up_t = _tile_gateup(np.asarray(L1_ffn_up_w, dtype=np.float32))
    down_t = np.ascontiguousarray(np.asarray(L1_ffn_down_w, dtype=np.float32).T)
    gs = np.ascontiguousarray(
        np.asarray(L1_ffn_gate_s, dtype=np.float32).reshape(44, 128).T)
    us = np.ascontiguousarray(
        np.asarray(L1_ffn_up_s, dtype=np.float32).reshape(44, 128).T)
    ds = np.ascontiguousarray(np.asarray(L1_ffn_down_s, dtype=np.float32))
    in_maps = [{"x3": x3[b], "gate_t": gate_t, "up_t": up_t, "down_t": down_t,
                "gs": gs, "us": us, "ds": ds, "ident": ident}
               for b in range(NCORES)]
    res = run_bass_kernel_spmd(progs["d"], in_maps, core_ids=cores)
    x4 = [np.asarray(r["x4"]) for r in res.results]
    return np.stack(x4).astype(np.float32)



def build_phase_b(nc, ctx, P, S, x1_in, factors_in, bias_in, x2_out, ident_in,
                  fsum_pre=None):
    t, v, sc = S['t'], S['v'], S['s']
    qmi, gms = S['qmi'], S['gms']
    qx = [S['qx0'], S['qx1']]
    qst = [S['qst0'], S['qst1']]
    gfc = [S['gfc0'], S['gfc1'], S['gfc2'], S['gfc3']]

    id_sb = ctx.enter_context(nc.sbuf_tensor("id_sbB", [128, 128], BF16))
    xbuf = ctx.enter_context(nc.sbuf_tensor("xbufB", [128, 2 * D], F32))
    trash = ctx.enter_context(nc.sbuf_tensor("trashB", [128, D], F32))
    s1 = ctx.enter_context(nc.sbuf_tensor("s1B", [128, NT], F32))
    s2 = ctx.enter_context(nc.sbuf_tensor("s2B", [128, NT], F32))
    rstd = ctx.enter_context(nc.sbuf_tensor("rstdB", [128, NT], F32))
    nbias = ctx.enter_context(nc.sbuf_tensor("nbiasB", [128, NT], F32))
    tmp = ctx.enter_context(nc.sbuf_tensor("tmpB", [128, 3 * NT], F32))
    facc = ctx.enter_context(nc.sbuf_tensor("facc", [128, 3 * D], F32))
    f_sb = ctx.enter_context(nc.sbuf_tensor("f_sb", [128, ND * D], BF16))
    m_bf = ctx.enter_context(nc.sbuf_tensor("m_bf", [128, NT * D], BF16))
    zrow = ctx.enter_context(nc.sbuf_tensor("zrow", [128, 2 * D], BF16))
    zt = ctx.enter_context(nc.sbuf_tensor("ztB", [128, 2 * D], BF16))
    ones8 = ctx.enter_context(nc.sbuf_tensor("ones8", [1, 128], F32))
    brow = ctx.enter_context(nc.sbuf_tensor("brow", [1, D], F32))
    bias8 = ctx.enter_context(nc.sbuf_tensor("bias8", [128, D], BF16))

    ps_m = ctx.enter_context(nc.psum_tensor("ps_m", [128, 2048], F32))
    ps_t0 = ctx.enter_context(nc.psum_tensor("ps_t0B", [128, 1024], BF16))
    ps_t1 = ctx.enter_context(nc.psum_tensor("ps_t1B", [128, 1024], BF16))
    ps_tt = [ps_t0, ps_t1]
    ps_b = ctx.enter_context(nc.psum_tensor("ps_bB", [128, 512], F32))

    if fsum_pre is not None:
        fsum_ap, fsum_wait = fsum_pre
        gj = gfc[1]

        def fld(e, fw=fsum_wait, gh=gj.h):
            e.wait_ge(fw[0], fw[1])
            e.dma_start(out=f_sb[:, :].rearrange("p (dt c) -> p dt c", c=D),
                        in_=fsum_ap.rearrange("(dt p) c -> p dt c", p=128)
                        ).then_inc(gh, 16)
        P.g(fld)
        f_ready = [(gj.h, gj.inc(16))]
        emit_fchain = False
    else:
        emit_fchain = True
    fcast_g = []
    for dt in range(ND if emit_fchain else 0):
        j = dt % 3
        gj = gfc[j]
        base = gj.n + 16

        def fchain(e, dt=dt, j=j, fg=list(fcast_g), base=base, gh=gj.h):
            if dt >= 3:
                e.wait_ge(gh, fg[dt - 3])
            e.dma_start(out=facc[:, j * D:(j + 1) * D],
                        in_=factors_in[0, dt * 128:(dt + 1) * 128, :]
                        ).then_inc(gh, 16)
            for r in range(1, 8):
                e.wait_ge(gh, base + (r - 1) * 16)
                e.dma_start(out=facc[:, j * D:(j + 1) * D],
                            in_=factors_in[r, dt * 128:(dt + 1) * 128, :],
                            accum_op=ALU.add).then_inc(gh, 16)
            e.wait_ge(gh, base + 7 * 16)
            e.dma_start(out=f_sb[:, dt * D:(dt + 1) * D],
                        in_=facc[:, j * D:(j + 1) * D]).then_inc(gh, 16)
        P.g(fchain)
        gj.inc(16 * 9)
        fcast_g.append(gj.n)
    if emit_fchain:
        f_ready = [(s_.h, s_.n) for s_ in gfc]

    def ld_idB(e):
        e.dma_start(out=id_sb[:, :], in_=ident_in).then_inc(qmi.h, 16)
        e.dma_start(out=brow[0:1, :], in_=bias_in[None, :]).then_inc(qmi.h, 16)
    P.y(ld_idB)
    idb_ready = qmi.inc(32)

    def mset(e):
        e.memset(ones8[:, :], float(np.sqrt(8.0)))
    P.g(mset)

    def mset_bar(e):
        e.dma_start(out=ones8[0:1, 0:8], in_=ones8[0:1, 0:8]).then_inc(gms.h, 16)
    P.g(mset_bar)
    ones_ready = gms.inc(16)

    bev_v = []
    for et in range(4):
        def bbc(e, et=et, idb=idb_ready, onr=ones_ready, bv=list(bev_v)):
            if et == 0:
                e.wait_ge(qmi.h, idb)
                e.wait_ge(gms.h, onr)
            else:
                e.wait_ge(v.h, bv[-1])
            e.matmul(ps_b[:, :], ones8[0:1, :],
                     brow[0:1, et * 512:(et + 1) * 512],
                     start=True, stop=True).then_inc(t.h, 1)
        P.t(bbc)
        btv = t.inc()

        def bev(e, et=et, btv=btv):
            e.wait_ge(t.h, btv)
            e.tensor_copy(bias8[:, et * 512:(et + 1) * 512],
                          ps_b[:, :]).then_inc(v.h, 1)
        P.v(bev)
        bev_v.append(v.inc())
    bias8_v = bev_v[-1]

    statB_v, statB_s = [], []
    for k in range(NT):
        j = k % 2

        def ld(e, k=k, j=j, sv=list(statB_v), ss=list(statB_s), qh=qx[j].h):
            if k >= 2:
                e.wait_ge(v.h, sv[k - 2])
                e.wait_ge(sc.h, ss[k - 2])
            e.dma_start(out=xbuf[:, j * D:(j + 1) * D],
                        in_=x1_in[k * 128:(k + 1) * 128, :]).then_inc(qh, 16)
        P.y(ld)
        dw = (qx[j].h, qx[j].inc(16))
        vv, sv_ = emit_stats_tile(P, S, xbuf[:, j * D:(j + 1) * D], k,
                                  s1, s2, trash[:, :], dw)
        statB_v.append(vv)
        statB_s.append(sv_)

    lnB = emit_ln_finalize(P, S, s1, s2, rstd, nbias, tmp, D, EPS, NT,
                           wait=[(v.h, statB_v[-1]), (sc.h, statB_s[-1])])

    mstat_v, mstat_s = [], []
    mev_v = []
    norm_sB = []
    tp_vB = []
    for k in range(NT):
        j = k % 2

        def ldx(e, k=k, j=j, ns=list(norm_sB), qh=qx[j].h,
                pv=statB_v[-1], ps=statB_s[-1]):
            if k >= 2:
                e.wait_ge(sc.h, ns[k - 2])
            else:
                e.wait_ge(v.h, pv)
                e.wait_ge(sc.h, ps)
            e.dma_start(out=xbuf[:, j * D:(j + 1) * D],
                        in_=x1_in[k * 128:(k + 1) * 128, :]).then_inc(qh, 16)
        P.y(ldx)
        dxw = (qx[j].h, qx[j].inc(16))

        def nrm(e, k=k, j=j, dxw=dxw, lnB=lnB, tv=list(tp_vB)):
            e.wait_ge(dxw[0], dxw[1])
            e.wait_ge(v.h, lnB)
            if k >= 2:
                e.wait_ge(v.h, tv[-3])
            e.activation(zrow[:, j * D:(j + 1) * D],
                         xbuf[:, j * D:(j + 1) * D],
                         AF.Identity, bias=nbias[:, k:k + 1],
                         scale=rstd[:, k:k + 1]).then_inc(sc.h, 1)
        P.s(nrm)
        norm_sB.append(sc.inc())

        for grp in range(2):
            pst = ps_tt[grp]

            def tp(e, k=k, j=j, grp=grp, pst=pst, nsv=norm_sB[-1],
                   idb=idb_ready, tv=list(tp_vB)):
                e.wait_ge(sc.h, nsv)
                if k == 0 and grp == 0:
                    e.wait_ge(qmi.h, idb)
                if len(tv) >= 2:
                    e.wait_ge(v.h, tv[-2])
                for jj in range(8):
                    dt = grp * 8 + jj
                    ins = e.transpose(
                        pst[:, jj * 128:(jj + 1) * 128],
                        zrow[:, j * D + dt * 128:j * D + (dt + 1) * 128],
                        id_sb[:, :])
                    if jj == 7:
                        ins.then_inc(t.h, 1)
            P.t(tp)
            tval = t.inc()

            def ev(e, j=j, grp=grp, pst=pst, tval=tval):
                e.wait_ge(t.h, tval)
                for jj in range(8):
                    dt = grp * 8 + jj
                    ins = e.tensor_copy(
                        zt[:, j * D + dt * 128:j * D + (dt + 1) * 128],
                        pst[:, jj * 128:(jj + 1) * 128])
                    if jj == 7:
                        ins.then_inc(v.h, 1)
            P.v(ev)
            tp_vB.append(v.inc())

        def mm(e, k=k, j=j, ztv=tp_vB[-1], fr=tuple(f_ready), mv=list(mev_v)):
            e.wait_ge(v.h, ztv)
            for fh, fv in fr:
                e.wait_ge(fh, fv)
            if mv:
                e.wait_ge(v.h, mv[-1])
            for et in range(4):
                for dt in range(ND):
                    ins = e.matmul(
                        ps_m[:, et * 512:(et + 1) * 512],
                        zt[:, j * D + dt * 128:j * D + (dt + 1) * 128],
                        f_sb[:, dt * D + et * 512:dt * D + (et + 1) * 512],
                        start=(dt == 0), stop=(dt == ND - 1))
                    if et == 3 and dt == ND - 1:
                        ins.then_inc(t.h, 1)
        P.t(mm)
        mmv = t.inc()

        def mev(e, k=k, mmv=mmv, b8=bias8_v):
            e.wait_ge(t.h, mmv)
            e.wait_ge(v.h, b8)
            e.tensor_tensor(m_bf[:, k * D:(k + 1) * D], ps_m[:, :],
                            bias8[:, :], op=ALU.add).then_inc(v.h, 1)
        P.v(mev)
        mev_v.append(v.inc())

        def mst_v(e, k=k, mv=mev_v[-1]):
            e.wait_ge(v.h, mv)
            e.reduce_sum(s1[:, k:k + 1], m_bf[:, k * D:(k + 1) * D],
                         axis=AX.X).then_inc(v.h, 1)
        P.v(mst_v)
        mstat_v.append(v.inc())

        def mst_s(e, k=k, mv=mev_v[-1]):
            e.wait_ge(v.h, mv)
            e.activation(trash[:, :], m_bf[:, k * D:(k + 1) * D], AF.Square,
                         accum_out=s2[:, k:k + 1]).then_inc(sc.h, 1)
        P.s(mst_s)
        mstat_s.append(sc.inc())

    lnM = emit_ln_finalize(P, S, s1, s2, rstd, nbias, tmp, D, 8 * EPS, NT,
                           wait=[(v.h, mstat_v[-1]), (sc.h, mstat_s[-1])])

    st2_q = []
    addB_v = []
    nsl = norm_sB[-1]
    for k in range(NT):
        j = k % 2

        def ldx(e, k=k, j=j, sq=list(st2_q), nsl=nsl, qh=qx[j].h,
                sh=qst[j].h):
            if k >= 2:
                e.wait_ge(sh, sq[k - 2])
            else:
                e.wait_ge(sc.h, nsl)
            e.dma_start(out=xbuf[:, j * D:(j + 1) * D],
                        in_=x1_in[k * 128:(k + 1) * 128, :]).then_inc(qh, 16)
        P.y(ldx)
        dxw = (qx[j].h, qx[j].inc(16))

        def nrmM(e, k=k, lnM=lnM, av=list(addB_v)):
            e.wait_ge(v.h, lnM)
            if av:
                e.wait_ge(v.h, av[-1])
            e.activation(trash[:, :], m_bf[:, k * D:(k + 1) * D], AF.Identity,
                         bias=nbias[:, k:k + 1],
                         scale=rstd[:, k:k + 1]).then_inc(sc.h, 1)
        P.s(nrmM)
        nv = sc.inc()

        def addB(e, k=k, j=j, nv=nv, dxw=dxw):
            e.wait_ge(sc.h, nv)
            e.wait_ge(dxw[0], dxw[1])
            e.tensor_tensor(xbuf[:, j * D:(j + 1) * D],
                            xbuf[:, j * D:(j + 1) * D], trash[:, :],
                            op=ALU.add).then_inc(v.h, 1)
        P.v(addB)
        addB_v.append(v.inc())

        def stx(e, k=k, j=j, av=addB_v[-1], sh=qst[j].h):
            e.wait_ge(v.h, av)
            e.dma_start(out=x2_out[k * 128:(k + 1) * 128, :],
                        in_=xbuf[:, j * D:(j + 1) * D]).then_inc(sh, 16)
        P.y(stx)
        st2_q.append(qst[j].inc(16))

    def finB(e, l0=qst[0].n, l1=qst[1].n):
        e.wait_ge(qst[0].h, l0)
        e.wait_ge(qst[1].h, l1)
    P.y(finB)


def build_b_program():
    nc = bass.Bass("TRN2", target_bir_lowering=False)
    x1_in = nc.dram_tensor("x1", [T, D], F32, kind="ExternalInput").ap()
    facs = nc.dram_tensor("factors", [8, D, D], F32, kind="ExternalInput").ap()
    bias_in = nc.dram_tensor("fbias", [D], F32, kind="ExternalInput").ap()
    ident_in = nc.dram_tensor("ident", [128, 128], BF16,
                              kind="ExternalInput").ap()
    x2_out = nc.dram_tensor("x2", [T, D], F32, kind="ExternalOutput").ap()
    with contextlib.ExitStack() as ctx:
        S = mk_sems(nc, ctx)
        P = Prog(nc)
        build_phase_b(nc, ctx, P, S, x1_in, facs, bias_in, x2_out, ident_in)
        P.emit()
    return nc



H = 32
DH = 64
G = 2


def build_phase_c(nc, ctx, P, S, x2_in, wqt_in, wkt_in, wvt_in, wot_in,
                  ns_in, ss_in, x3_out, ident_in):
    t, v, sc = S['t'], S['v'], S['s']
    qmi, gms, gkv, gwq, gwo = (S['qmi'], S['gms'], S['gkv'], S['gwq'],
                               S['gwo'])
    qx = [S['qx0'], S['qx1']]
    qst = [S['qst0'], S['qst1']]

    id_sb = ctx.enter_context(nc.sbuf_tensor("id_sbC", [128, 128], BF16))
    xbuf = ctx.enter_context(nc.sbuf_tensor("xbufC", [128, 2 * D], F32))
    trash = ctx.enter_context(nc.sbuf_tensor("trashC", [128, D], F32))
    zres = ctx.enter_context(nc.sbuf_tensor("zresC", [128, D], F32))
    s1 = ctx.enter_context(nc.sbuf_tensor("s1C", [128, NT], F32))
    s2 = ctx.enter_context(nc.sbuf_tensor("s2C", [128, NT], F32))
    rstd = ctx.enter_context(nc.sbuf_tensor("rstdC", [128, NT], F32))
    nbias = ctx.enter_context(nc.sbuf_tensor("nbiasC", [128, NT], F32))
    tmp = ctx.enter_context(nc.sbuf_tensor("tmpC", [128, 5 * NT], F32))
    rncol = ctx.enter_context(nc.sbuf_tensor("rncol", [128, NT], F32))
    rnr = ctx.enter_context(nc.sbuf_tensor("rnr", [128, NT], F32))
    rnb = ctx.enter_context(nc.sbuf_tensor("rnb", [128, NT], F32))
    nsss = ctx.enter_context(nc.sbuf_tensor("nsss", [128, ND], F32))
    poolA = ctx.enter_context(nc.sbuf_tensor("poolA", [128, ND * T], BF16))
    poolB = ctx.enter_context(nc.sbuf_tensor("poolB", [128, ND * T], BF16))
    esb = ctx.enter_context(nc.sbuf_tensor("esbC", [128, NT * 512], BF16))
    kt = ctx.enter_context(nc.sbuf_tensor("ktC", [128, T], BF16))
    vsb = ctx.enter_context(nc.sbuf_tensor("vsbC", [128, NT * 130], BF16))
    rs_sb = ctx.enter_context(nc.sbuf_tensor("rs_sb", [128, 1024], F32))
    ones64 = ctx.enter_context(nc.sbuf_tensor("ones64", [128, 64], F32))
    wctx = contextlib.ExitStack()
    kvw = wctx.enter_context(nc.sbuf_tensor("kvw", [128, ND * 256], BF16))
    wqs = wctx.enter_context(nc.sbuf_tensor("wqs", [128, ND * 256], BF16))

    def ld0(e):
        e.dma_start(out=id_sb[:, :], in_=ident_in).then_inc(qmi.h, 16)
        e.dma_start(out=nsss[:, :], in_=ns_in).then_inc(qmi.h, 16)
        e.dma_start(out=tmp[:, 4 * NT:5 * NT], in_=ss_in).then_inc(qmi.h, 16)
    P.y(ld0)
    c0_q = qmi.inc(48)

    def mset1(e):
        e.memset(ones64[:, :], 1.0)
        e.memset(vsb[:, :], 1.0)
        e.dma_start(out=ones64[0:1, 0:8], in_=ones64[0:1, 0:8]).then_inc(gms.h, 16)
    P.g(mset1)
    mset_g = gms.inc(16)

    def nsmul(e, c0=c0_q):
        e.wait_ge(qmi.h, c0)
        e.tensor_mul(nsss[:, :], nsss[:, :],
                     tmp[:, 4 * NT:5 * NT]).then_inc(v.h, 1)
    P.v(nsmul)
    nsss_v = v.inc()

    def ldkv(e, nv=nsss_v):
        src_k = wkt_in.rearrange("(dt p) c -> p dt c", p=128)
        src_v = wvt_in.rearrange("(dt p) c -> p dt c", p=128)
        dst = kvw[:, :].rearrange("p (dt c) -> p dt c", c=256)
        e.dma_start(out=dst[:, :, 0:128], in_=src_k).then_inc(gkv.h, 16)
        e.dma_start(out=dst[:, :, 128:256], in_=src_v).then_inc(gkv.h, 16)
    P.g(ldkv)
    kv_g = gkv.inc(32)

    def sckv(e, kv_g=kv_g, nv=nsss_v):
        e.wait_ge(gkv.h, kv_g)
        e.wait_ge(v.h, nv)
        for dt in range(ND):
            ins = e.tensor_scalar_mul(kvw[:, dt * 256:(dt + 1) * 256],
                                      kvw[:, dt * 256:(dt + 1) * 256],
                                      nsss[:, dt:dt + 1])
            if dt == ND - 1:
                ins.then_inc(v.h, 1)
    P.v(sckv)
    kvw_v = v.inc()

    statC_v, statC_s = [], []
    for k in range(NT):
        j = k % 2

        def ld(e, k=k, j=j, sv=list(statC_v), ss=list(statC_s), qh=qx[j].h):
            if k >= 2:
                e.wait_ge(v.h, sv[k - 2])
                e.wait_ge(sc.h, ss[k - 2])
            e.dma_start(out=xbuf[:, j * D:(j + 1) * D],
                        in_=x2_in[k * 128:(k + 1) * 128, :]).then_inc(qh, 16)
        P.y(ld)
        dw = (qx[j].h, qx[j].inc(16))
        vv, sv_ = emit_stats_tile(P, S, xbuf[:, j * D:(j + 1) * D], k,
                                  s1, s2, trash[:, :], dw)
        statC_v.append(vv)
        statC_s.append(sv_)

    lnC = emit_ln_finalize(P, S, s1, s2, rstd, nbias, tmp, D, EPS, NT,
                           wait=[(v.h, statC_v[-1]), (sc.h, statC_s[-1])])

    varr = tmp[:, 3 * NT:4 * NT]

    def rn1(e, lnC=lnC):
        e.wait_ge(v.h, lnC)
        e.tensor_scalar_add(varr, tmp[:, NT:2 * NT],
                            -float(EPS)).then_inc(v.h, 1)
    P.v(rn1)
    rn1v = v.inc()

    def rn2(e, rn1v=rn1v):
        e.wait_ge(v.h, rn1v)
        e.activation(varr, varr, AF.Sqrt, bias=0.0, scale=1.0).then_inc(sc.h, 1)
    P.s(rn2)
    rn2s = sc.inc()

    def rn3(e, rn2s=rn2s):
        e.wait_ge(sc.h, rn2s)
        e.tensor_mul(varr, varr, rstd[:, 0:NT]).then_inc(v.h, 1)
    P.v(rn3)
    rn3v = v.inc()

    def rn4(e, rn3v=rn3v):
        e.wait_ge(v.h, rn3v)
        e.tensor_scalar_add(varr, varr, 1e-8).then_inc(v.h, 1)
    P.v(rn4)
    rn4v = v.inc()

    def rn5(e, rn4v=rn4v):
        e.wait_ge(v.h, rn4v)
        e.reciprocal(rncol[:, :], varr).then_inc(v.h, 1)
    P.v(rn5)
    rn5v = v.inc()

    def rn6(e, rn5v=rn5v):
        e.wait_ge(v.h, rn5v)
        e.tensor_mul(rnr[:, :], rncol[:, :], rstd[:, 0:NT])
        e.tensor_mul(rnb[:, :], rncol[:, :], nbias[:, 0:NT]).then_inc(v.h, 1)
    P.v(rn6)
    rn_ready = v.inc()

    with contextlib.ExitStack() as pctx:
        ps_t0 = pctx.enter_context(nc.psum_tensor("ps_t0C", [128, 1024], BF16))
        ps_t1 = pctx.enter_context(nc.psum_tensor("ps_t1C", [128, 1024], BF16))
        ps_tt = [ps_t0, ps_t1]
        ps_q0 = pctx.enter_context(nc.psum_tensor("ps_q0C", [128, 512], F32))
        ps_q1 = pctx.enter_context(nc.psum_tensor("ps_q1C", [128, 512], F32))
        ps_qq = [ps_q0, ps_q1]

        zh_s = []
        tpC_v = []
        for k in range(NT):
            j = k % 2

            def ldx(e, k=k, j=j, zs=list(zh_s), qh=qx[j].h,
                    pv=statC_v[-1], ps=statC_s[-1]):
                if k >= 2:
                    e.wait_ge(sc.h, zs[k - 2])
                else:
                    e.wait_ge(v.h, pv)
                    e.wait_ge(sc.h, ps)
                e.dma_start(out=xbuf[:, j * D:(j + 1) * D],
                            in_=x2_in[k * 128:(k + 1) * 128, :]
                            ).then_inc(qh, 16)
            P.y(ldx)
            dxw = (qx[j].h, qx[j].inc(16))

            def zh(e, k=k, j=j, dxw=dxw, rr=rn_ready, tv=list(tpC_v)):
                e.wait_ge(dxw[0], dxw[1])
                e.wait_ge(v.h, rr)
                if k >= 2:
                    e.wait_ge(v.h, tv[-3])
                e.activation(esb[:, j * D:(j + 1) * D],
                             xbuf[:, j * D:(j + 1) * D],
                             AF.Identity, bias=rnb[:, k:k + 1],
                             scale=rnr[:, k:k + 1]).then_inc(sc.h, 1)
            P.s(zh)
            zh_s.append(sc.inc())

            for grp in range(2):
                pst = ps_tt[grp]

                def tp(e, k=k, j=j, grp=grp, pst=pst, zs=zh_s[-1],
                       c0=c0_q, tv=list(tpC_v)):
                    e.wait_ge(sc.h, zs)
                    if k == 0 and grp == 0:
                        e.wait_ge(qmi.h, c0)
                    if len(tv) >= 2:
                        e.wait_ge(v.h, tv[-2])
                    for jj in range(8):
                        dt = grp * 8 + jj
                        ins = e.transpose(
                            pst[:, jj * 128:(jj + 1) * 128],
                            esb[:, j * D + dt * 128:j * D + (dt + 1) * 128],
                            id_sb[:, :])
                        if jj == 7:
                            ins.then_inc(t.h, 1)
                P.t(tp)
                tval = t.inc()

                def ev(e, k=k, grp=grp, pst=pst, tval=tval):
                    e.wait_ge(t.h, tval)
                    for jj in range(8):
                        dt = grp * 8 + jj
                        ins = e.tensor_copy(
                            poolA[:, dt * T + k * 128:dt * T + (k + 1) * 128],
                            pst[:, jj * 128:(jj + 1) * 128])
                        if jj == 7:
                            ins.then_inc(v.h, 1)
                P.v(ev)
                tpC_v.append(v.inc())
        ht_v = tpC_v[-1]

        kev_v = []
        for s_ in range(4):
            par = s_ % 2

            def kmm(e, s_=s_, par=par, hv=ht_v, kv=kvw_v, kev=list(kev_v)):
                e.wait_ge(v.h, hv)
                e.wait_ge(v.h, kv)
                if len(kev) >= 2:
                    e.wait_ge(v.h, kev[-2])
                for dt in range(ND):
                    ins = e.matmul(
                        ps_qq[par][:, :],
                        kvw[:, dt * 256:dt * 256 + 128],
                        poolA[:, dt * T + s_ * 512:dt * T + (s_ + 1) * 512],
                        start=(dt == 0), stop=(dt == ND - 1))
                    if dt == ND - 1:
                        ins.then_inc(t.h, 1)
            P.t(kmm)
            ktv = t.inc()

            def kev_(e, s_=s_, par=par, ktv=ktv):
                e.wait_ge(t.h, ktv)
                e.tensor_copy(kt[:, s_ * 512:(s_ + 1) * 512],
                              ps_qq[par][:, :]).then_inc(v.h, 1)
            P.v(kev_)
            kev_v.append(v.inc())

        vev_v = []
        for k in range(NT):
            par = k % 2

            def vmm(e, k=k, par=par, hv=ht_v, kv=kvw_v, kl=kev_v[-1],
                    vv=list(vev_v)):
                if k == 0:
                    e.wait_ge(v.h, kl)
                e.wait_ge(v.h, hv)
                e.wait_ge(v.h, kv)
                if len(vv) >= 2:
                    e.wait_ge(v.h, vv[-2])
                for dt in range(ND):
                    ins = e.matmul(
                        ps_qq[par][:, 0:128],
                        poolA[:, dt * T + k * 128:dt * T + (k + 1) * 128],
                        kvw[:, dt * 256 + 128:(dt + 1) * 256],
                        start=(dt == 0), stop=(dt == ND - 1))
                    if dt == ND - 1:
                        ins.then_inc(t.h, 1)
            P.t(vmm)
            vtv = t.inc()

            def vev(e, k=k, par=par, vtv=vtv, mg=mset_g):
                e.wait_ge(t.h, vtv)
                e.wait_ge(gms.h, mg)
                e.tensor_copy(vsb[:, k * 130:k * 130 + 64],
                              ps_qq[par][:, 0:64])
                e.tensor_copy(vsb[:, k * 130 + 65:k * 130 + 129],
                              ps_qq[par][:, 64:128]).then_inc(v.h, 1)
            P.v(vev)
            vev_v.append(v.inc())
        v_ready = vev_v[-1]

        qev_v = []
        for sg in range(8):
            def ldwq(e, sg=sg, nv=nsss_v, qe=list(qev_v)):
                if qe:
                    e.wait_ge(v.h, qe[-1])
                src = wqt_in[:, sg * 256:(sg + 1) * 256]
                src = src.rearrange("(dt p) c -> p dt c", p=128)
                e.dma_start(out=wqs[:, :].rearrange("p (dt c) -> p dt c", c=256),
                            in_=src).then_inc(gwq.h, 16)
            P.g(ldwq)
            wq_g = gwq.inc(16)

            def scwq(e, wq_g=wq_g, nv=nsss_v):
                e.wait_ge(gwq.h, wq_g)
                e.wait_ge(v.h, nv)
                for dt in range(ND):
                    ins = e.tensor_scalar_mul(wqs[:, dt * 256:(dt + 1) * 256],
                                              wqs[:, dt * 256:(dt + 1) * 256],
                                              nsss[:, dt:dt + 1])
                    if dt == ND - 1:
                        ins.then_inc(v.h, 1)
            P.v(scwq)
            wqs_v = v.inc()

            for mt_l in range(2):
                mt = sg * 2 + mt_l
                for s_ in range(4):
                    par = (mt_l * 4 + s_) % 2

                    def qmm(e, mt=mt, mt_l=mt_l, s_=s_, par=par, wv=wqs_v,
                            hv=ht_v, vl=v_ready, qe=list(qev_v)):
                        e.wait_ge(v.h, wv)
                        e.wait_ge(v.h, hv)
                        if len(qe) < 2:
                            e.wait_ge(v.h, vl)
                        else:
                            e.wait_ge(v.h, qe[-2])
                        for dt in range(ND):
                            ins = e.matmul(
                                ps_qq[par][:, :],
                                wqs[:, dt * 256 + mt_l * 128:
                                    dt * 256 + (mt_l + 1) * 128],
                                poolA[:, dt * T + s_ * 512:dt * T + (s_ + 1) * 512],
                                start=(dt == 0), stop=(dt == ND - 1))
                            if dt == ND - 1:
                                ins.then_inc(t.h, 1)
                    P.t(qmm)
                    qtv = t.inc()

                    def qev(e, mt=mt, s_=s_, par=par, qtv=qtv):
                        e.wait_ge(t.h, qtv)
                        for half in range(2):
                            h_ = 2 * mt + half
                            u = h_ % 16
                            gb = 64 * (h_ // 16)
                            ins = e.tensor_copy(
                                poolB[gb:gb + 64,
                                      u * T + s_ * 512:u * T + (s_ + 1) * 512],
                                ps_qq[par][half * 64:half * 64 + 64, :])
                            if half == 1:
                                ins.then_inc(v.h, 1)
                    P.v(qev)
                    qev_v.append(v.inc())
        q_ready = qev_v[-1]
    wctx.close()
    esb2 = ctx.enter_context(nc.sbuf_tensor("esb2C", [128, NT * 512], BF16))
    esbs = [esb, esb2]

    with contextlib.ExitStack() as pctx:
        ps_sc0 = pctx.enter_context(nc.psum_tensor("ps_sc0", [128, 1024], F32))
        ps_sc1 = pctx.enter_context(nc.psum_tensor("ps_sc1", [128, 1024], F32))
        ps_scc = [ps_sc0, ps_sc1]
        ps_av0 = pctx.enter_context(nc.psum_tensor("ps_av0", [128, 512], F32))
        ps_av1 = pctx.enter_context(nc.psum_tensor("ps_av1", [128, 512], F32))
        ps_avv = [ps_av0, ps_av1]
        ps_rep = pctx.enter_context(nc.psum_tensor("ps_rep", [128, 512], F32))

        exp_s = []
        last_exp = []
        av_t = []
        rec_vs = []
        rep_t = []
        norm_v = []
        wot_g = []

        hc_list = [(h_, c_) for h_ in range(H) for c_ in range(4)]
        NIDX = len(hc_list)

        def emit_scores(idx):
            h_, c_ = hc_list[idx]
            g_ = h_ // 16
            kb = g_ * 64
            for tg in range(8):
                par = tg % 2

                def smm(e, h_=h_, c_=c_, tg=tg, par=par, kb=kb,
                        qr=q_ready, es=list(exp_s), idx=idx,
                        av=list(av_t)):
                    if idx == 0 and tg == 0:
                        e.wait_ge(v.h, qr)
                    if len(es) >= 2:
                        e.wait_ge(sc.h, es[-2])
                    u = h_ % 16
                    for half in range(2):
                        tt = tg * 2 + half
                        ins = e.matmul(
                            ps_scc[par][:, half * 512:(half + 1) * 512],
                            kt[kb:kb + 64, tt * 128:(tt + 1) * 128],
                            poolB[kb:kb + 64,
                                  u * T + c_ * 512:u * T + (c_ + 1) * 512],
                            start=True, stop=True)
                        if half == 1:
                            ins.then_inc(t.h, 1)
                P.t(smm)
                stv = t.inc()

                def exf(e, tg=tg, par=par, stv=stv, idx=idx,
                        eb=esbs[idx % 2], av=list(av_t)):
                    e.wait_ge(t.h, stv)
                    if tg == 0 and idx >= 2:
                        e.wait_ge(t.h, av[idx - 2])
                    e.activation(eb[:, tg * 1024:(tg + 1) * 1024],
                                 ps_scc[par][:, :], AF.Exp,
                                 bias=0.0, scale=0.125).then_inc(sc.h, 1)
                P.s(exf)
                exp_s.append(sc.inc())
            last_exp.append(exp_s[-1])
            if h_ >= 16 and c_ == 3:
                def ldwo(e, u=h_ - 16, stv=stv):
                    e.wait_ge(t.h, stv)
                    e.dma_start(out=poolB[:, u * T:(u + 1) * T],
                                in_=wot_in[u * 128:(u + 1) * 128, :]
                                ).then_inc(gwo.h, 16)
                P.g(ldwo)
                wot_g.append(gwo.inc(16))

        def emit_av(idx):
            h_, c_ = hc_list[idx]
            g_ = h_ // 16
            mt = h_ // 2
            qb = (h_ % 2) * 64
            par_av = idx % 2
            vcol = 65 * g_

            def avm(e, idx=idx, par_av=par_av, vcol=vcol, pe=last_exp[idx],
                    nv=list(norm_v), vl=v_ready, eb=esbs[idx % 2]):
                e.wait_ge(sc.h, pe)
                e.wait_ge(v.h, vl)
                if len(nv) >= 2:
                    e.wait_ge(v.h, nv[-2])
                for tt in range(NT):
                    ins = e.matmul(
                        ps_avv[par_av][0:65, :],
                        vsb[:, tt * 130 + vcol:tt * 130 + vcol + 65],
                        eb[:, tt * 512:(tt + 1) * 512],
                        start=(tt == 0), stop=(tt == NT - 1))
                    if tt == NT - 1:
                        ins.then_inc(t.h, 1)
            P.t(avm)
            av_t.append(t.inc())

            def rec(e, par_av=par_av, atv=av_t[-1]):
                e.wait_ge(t.h, atv)
                e.reciprocal(rs_sb[64:65, par_av * 512:(par_av + 1) * 512],
                             ps_avv[par_av][64:65, :]).then_inc(v.h, 1)
            P.v(rec)
            rec_vs.append(v.inc())

            def ocp(e, c_=c_, par_av=par_av, mt=mt, qb=qb, atv=av_t[-1]):
                e.wait_ge(t.h, atv)
                e.tensor_copy(
                    poolA[qb:qb + 64, mt * T + c_ * 512:mt * T + (c_ + 1) * 512],
                    ps_avv[par_av][0:64, :]).then_inc(v.h, 1)
            P.v(ocp)
            return v.inc()

        ocp_vs = {}

        def emit_rep(idx):
            h_, c_ = hc_list[idx]
            mt = h_ // 2
            qb = (h_ % 2) * 64

            def rep(e, rv=rec_vs[idx], nv=list(norm_v), par_av=idx % 2):
                e.wait_ge(v.h, rv)
                if len(nv) >= 1:
                    e.wait_ge(v.h, nv[-1])
                e.matmul(ps_rep[0:64, :], ones64[64:65, :],
                         rs_sb[64:65, par_av * 512:(par_av + 1) * 512],
                         start=True, stop=True).then_inc(t.h, 1)
            P.t(rep)
            rep_t.append(t.inc())

            def osc(e, c_=c_, mt=mt, qb=qb, ocv=ocp_vs[idx], rtv=rep_t[-1]):
                e.wait_ge(v.h, ocv)
                e.wait_ge(t.h, rtv)
                e.tensor_tensor(
                    poolA[qb:qb + 64, mt * T + c_ * 512:mt * T + (c_ + 1) * 512],
                    poolA[qb:qb + 64, mt * T + c_ * 512:mt * T + (c_ + 1) * 512],
                    ps_rep[0:64, :], op=ALU.mult).then_inc(v.h, 1)
            P.v(osc)
            norm_v.append(v.inc())

        for idx in range(NIDX + 2):
            if idx < NIDX:
                emit_scores(idx)
            if 1 <= idx <= NIDX:
                ocp_vs[idx - 1] = emit_av(idx - 1)
            if idx >= 2:
                emit_rep(idx - 2)
        o_ready = norm_v[-1]
        wot_ready = wot_g[-1]

    with contextlib.ExitStack() as pctx:
        ps_wo = pctx.enter_context(nc.psum_tensor("ps_wo", [128, 2048], F32))

        ep_v = []
        st3_q = []
        for k in range(NT):
            j = k % 2

            def ldx(e, k=k, j=j, sq=list(st3_q), zs=zh_s[-1], qh=qx[j].h,
                    sh=qst[j].h):
                if k >= 2:
                    e.wait_ge(sh, sq[k - 2])
                else:
                    e.wait_ge(sc.h, zs)
                e.dma_start(out=xbuf[:, j * D:(j + 1) * D],
                            in_=x2_in[k * 128:(k + 1) * 128, :]
                            ).then_inc(qh, 16)
            P.y(ldx)
            dxw = (qx[j].h, qx[j].inc(16))

            def womm(e, k=k, orv=o_ready, wr=wot_ready, ep=list(ep_v)):
                e.wait_ge(v.h, orv)
                e.wait_ge(gwo.h, wr)
                if ep:
                    e.wait_ge(v.h, ep[-1])
                for oc in range(4):
                    for dqt in range(ND):
                        ins = e.matmul(
                            ps_wo[:, oc * 512:(oc + 1) * 512],
                            poolA[:, dqt * T + k * 128:dqt * T + (k + 1) * 128],
                            poolB[:, dqt * T + oc * 512:dqt * T + (oc + 1) * 512],
                            start=(dqt == 0), stop=(dqt == ND - 1))
                        if oc == 3 and dqt == ND - 1:
                            ins.then_inc(t.h, 1)
            P.t(womm)
            wotv = t.inc()

            def zr(e, k=k, j=j, dxw=dxw, lnC=lnC, ep=list(ep_v)):
                e.wait_ge(dxw[0], dxw[1])
                e.wait_ge(v.h, lnC)
                if ep:
                    e.wait_ge(v.h, ep[-1])
                e.activation(zres[:, :], xbuf[:, j * D:(j + 1) * D],
                             AF.Identity, bias=nbias[:, k:k + 1],
                             scale=rstd[:, k:k + 1]).then_inc(sc.h, 1)
            P.s(zr)
            zrv = sc.inc()

            def aadd(e, k=k, wotv=wotv, zrv=zrv, ep=list(ep_v)):
                e.wait_ge(t.h, wotv)
                e.wait_ge(sc.h, zrv)
                if ep:
                    e.wait_ge(v.h, ep[-1])
                e.tensor_tensor(trash[:, :], zres[:, :], ps_wo[:, :],
                                op=ALU.add).then_inc(v.h, 1)
            P.v(aadd)
            aav = v.inc()

            def est_v(e, aav=aav):
                e.wait_ge(v.h, aav)
                e.reduce_sum(s1[:, 0:1], trash[:, :], axis=AX.X).then_inc(v.h, 1)
            P.v(est_v)
            e1v = v.inc()

            def est_s(e, aav=aav):
                e.wait_ge(v.h, aav)
                e.activation(zres[:, :], trash[:, :], AF.Square,
                             accum_out=s2[:, 0:1]).then_inc(sc.h, 1)
            P.s(est_s)
            e1s = sc.inc()

            lnE = emit_ln_finalize(P, S, s1, s2, rstd, nbias, tmp, D, EPS, 1,
                                   wait=[(v.h, e1v), (sc.h, e1s)])

            def nrmE(e, k=k, lnE=lnE):
                e.wait_ge(v.h, lnE)
                e.activation(zres[:, :], trash[:, :], AF.Identity,
                             bias=nbias[:, 0:1],
                             scale=rstd[:, 0:1]).then_inc(sc.h, 1)
            P.s(nrmE)
            nev = sc.inc()

            def addE(e, k=k, j=j, nev=nev):
                e.wait_ge(sc.h, nev)
                e.tensor_tensor(xbuf[:, j * D:(j + 1) * D],
                                xbuf[:, j * D:(j + 1) * D], zres[:, :],
                                op=ALU.add).then_inc(v.h, 1)
            P.v(addE)
            ep_v.append(v.inc())

            def stx(e, k=k, j=j, av=ep_v[-1], sh=qst[j].h):
                e.wait_ge(v.h, av)
                e.dma_start(out=x3_out[k * 128:(k + 1) * 128, :],
                            in_=xbuf[:, j * D:(j + 1) * D]).then_inc(sh, 16)
            P.y(stx)
            st3_q.append(qst[j].inc(16))

        def finC(e, l0=qst[0].n, l1=qst[1].n):
            e.wait_ge(qst[0].h, l0)
            e.wait_ge(qst[1].h, l1)
        P.y(finC)


def build_c_program():
    nc = bass.Bass("TRN2", target_bir_lowering=False)
    x2_in = nc.dram_tensor("x2", [T, D], F32, kind="ExternalInput").ap()
    wqt = nc.dram_tensor("wqt", [D, D], F32, kind="ExternalInput").ap()
    wkt = nc.dram_tensor("wkt", [D, 128], F32, kind="ExternalInput").ap()
    wvt = nc.dram_tensor("wvt", [D, 128], F32, kind="ExternalInput").ap()
    wot = nc.dram_tensor("wot", [D, D], F32, kind="ExternalInput").ap()
    nsc = nc.dram_tensor("nscol", [128, ND], F32, kind="ExternalInput").ap()
    ssc = nc.dram_tensor("sscol", [128, ND], F32, kind="ExternalInput").ap()
    ident_in = nc.dram_tensor("ident", [128, 128], BF16,
                              kind="ExternalInput").ap()
    x3_out = nc.dram_tensor("x3", [T, D], F32, kind="ExternalOutput").ap()
    with contextlib.ExitStack() as ctx:
        S = mk_sems(nc, ctx)
        P = Prog(nc)
        build_phase_c(nc, ctx, P, S, x2_in, wqt, wkt, wvt, wot, nsc, ssc,
                      x3_out, ident_in)
        P.emit()
    return nc



NI = 44
INTER = 5632


def build_phase_d(nc, ctx, P, S, x3_in, gate_t, up_t, down_t, gs_in, us_in,
                  ds_in, x4_out, ident_in, gate_b, up_b, down_b,
                  wcast_done=None):
    t, v, sc = S['t'], S['v'], S['s']
    qmi, gms, gmw = S['qmi'], S['gms'], S['gmw']
    qx = [S['qx0'], S['qx1']]
    qst = [S['qst0'], S['qst1']]
    gwg = [S['gwg0'], S['gwg1']]
    gwd = [S['gwd0'], S['gwd1'], S['gwd2'], S['gwd3']]

    id_sb = ctx.enter_context(nc.sbuf_tensor("id_sbD", [128, 128], BF16))
    xbuf = ctx.enter_context(nc.sbuf_tensor("xbufD", [128, 2 * D], F32))
    trash = ctx.enter_context(nc.sbuf_tensor("trashD", [128, D], F32))
    s1 = ctx.enter_context(nc.sbuf_tensor("s1D", [128, NT], F32))
    s2 = ctx.enter_context(nc.sbuf_tensor("s2D", [128, NT], F32))
    rstd = ctx.enter_context(nc.sbuf_tensor("rstdD", [128, NT], F32))
    nbias = ctx.enter_context(nc.sbuf_tensor("nbiasD", [128, NT], F32))
    tmp = ctx.enter_context(nc.sbuf_tensor("tmpD", [128, 3 * NT], F32))
    s1b = ctx.enter_context(nc.sbuf_tensor("s1bD", [128, 4], F32))
    s2b = ctx.enter_context(nc.sbuf_tensor("s2bD", [128, 4], F32))
    rstdb = ctx.enter_context(nc.sbuf_tensor("rstdbD", [128, 4], F32))
    nbiasb = ctx.enter_context(nc.sbuf_tensor("nbiasbD", [128, 4], F32))
    tmpb = ctx.enter_context(nc.sbuf_tensor("tmpbD", [128, 12], F32))
    hd = ctx.enter_context(nc.sbuf_tensor("hdD", [128, NI * 512], BF16))
    zt = ctx.enter_context(nc.sbuf_tensor("ztD", [128, ND * 512], BF16))
    zstage = ctx.enter_context(nc.sbuf_tensor("zstageD", [128, 2 * D], BF16))
    wg = ctx.enter_context(nc.sbuf_tensor("wgD", [128, 2 * D], BF16))
    wu = ctx.enter_context(nc.sbuf_tensor("wuD", [128, 2 * D], BF16))
    wd = ctx.enter_context(nc.sbuf_tensor("wdD", [128, 4 * 1024], BF16))
    och = ctx.enter_context(nc.sbuf_tensor("ochD", [128, 4 * D], F32))
    sgb = ctx.enter_context(nc.sbuf_tensor("sgbD", [128, 512], BF16))
    gsc = ctx.enter_context(nc.sbuf_tensor("gscD", [128, NI], F32))
    usc = ctx.enter_context(nc.sbuf_tensor("uscD", [128, NI], F32))
    dsrow = ctx.enter_context(nc.sbuf_tensor("dsrowD", [1, D], F32))
    onesD = ctx.enter_context(nc.sbuf_tensor("onesD", [1, 128], F32))
    dsrep = ctx.enter_context(nc.sbuf_tensor("dsrepD", [128, D], F32))

    ps_tp = ctx.enter_context(nc.psum_tensor("ps_tpD", [128, 1024], BF16))
    ps_g = ctx.enter_context(nc.psum_tensor("ps_gD", [128, 512], F32))
    ps_u = ctx.enter_context(nc.psum_tensor("ps_uD", [128, 512], F32))
    ps_o = [ctx.enter_context(nc.psum_tensor(f"ps_oD{i}", [128, 512], F32))
            for i in range(4)]
    ps_sp = ctx.enter_context(nc.psum_tensor("ps_spD", [128, 512], F32))

    if wcast_done is None:
        def wcast(e):
            e.dma_start(out=gate_b, in_=gate_t).then_inc(gmw.h, 16)
            e.dma_start(out=up_b, in_=up_t).then_inc(gmw.h, 16)
            e.dma_start(out=down_b, in_=down_t).then_inc(gmw.h, 16)
        P.g(wcast)
        wb_g = gmw.inc(48)
    else:
        wb_g = wcast_done

    def ld0(e):
        e.dma_start(out=id_sb[:, :], in_=ident_in).then_inc(qmi.h, 16)
        e.dma_start(out=gsc[:, :], in_=gs_in).then_inc(qmi.h, 16)
        e.dma_start(out=usc[:, :], in_=us_in).then_inc(qmi.h, 16)
        e.dma_start(out=dsrow[0:1, :], in_=ds_in[None, :]).then_inc(qmi.h, 16)
    P.y(ld0)
    c0_q = qmi.inc(64)

    def msetD(e):
        e.memset(onesD[:, :], 1.0)
        e.dma_start(out=onesD[0:1, 0:8], in_=onesD[0:1, 0:8]).then_inc(gms.h, 16)
    P.g(msetD)
    ones_g = gms.inc(16)

    dse_v = []
    for et in range(4):
        def dbc(e, et=et, c0=c0_q, og=ones_g, dv=list(dse_v)):
            if et == 0:
                e.wait_ge(qmi.h, c0)
                e.wait_ge(gms.h, og)
            else:
                e.wait_ge(v.h, dv[-1])
            e.matmul(ps_sp[:, :], onesD[0:1, :],
                     dsrow[0:1, et * 512:(et + 1) * 512],
                     start=True, stop=True).then_inc(t.h, 1)
        P.t(dbc)
        dtv = t.inc()

        def dbe(e, et=et, dtv=dtv):
            e.wait_ge(t.h, dtv)
            e.tensor_copy(dsrep[:, et * 512:(et + 1) * 512],
                          ps_sp[:, :]).then_inc(v.h, 1)
        P.v(dbe)
        dse_v.append(v.inc())
    ds_ready = dse_v[-1]

    statD_v, statD_s = [], []
    for k in range(NT):
        j = k % 2

        def ld(e, k=k, j=j, sv=list(statD_v), ss=list(statD_s), qh=qx[j].h):
            if k >= 2:
                e.wait_ge(v.h, sv[k - 2])
                e.wait_ge(sc.h, ss[k - 2])
            e.dma_start(out=xbuf[:, j * D:(j + 1) * D],
                        in_=x3_in[k * 128:(k + 1) * 128, :]).then_inc(qh, 16)
        P.y(ld)
        dw = (qx[j].h, qx[j].inc(16))
        vv, sv_ = emit_stats_tile(P, S, xbuf[:, j * D:(j + 1) * D], k,
                                  s1, s2, trash[:, :], dw)
        statD_v.append(vv)
        statD_s.append(sv_)

    lnD = emit_ln_finalize(P, S, s1, s2, rstd, nbias, tmp, D, EPS, NT,
                           wait=[(v.h, statD_v[-1]), (sc.h, statD_s[-1])])

    oev_all = []
    guev = []
    omm_t = []
    st4_q = []
    last_add_v = [None]
    last_zh_s = [statD_s[-1]]
    for tc in range(4):
        tpD_v = []
        zh_chunk = []
        for ti in range(4):
            k = tc * 4 + ti
            j = ti % 2

            def ldx(e, k=k, j=j, ti=ti, tc=tc, sq=list(st4_q),
                    zs0=last_zh_s[0], zv0=statD_v[-1], zc=list(zh_chunk),
                    qh=qx[j].h, sh=qst[j].h):
                if ti >= 2:
                    e.wait_ge(sc.h, zc[ti - 2])
                elif tc == 0:
                    e.wait_ge(sc.h, zs0)
                    e.wait_ge(v.h, zv0)
                elif len(sq) >= 2:
                    e.wait_ge(sh, sq[-2])
                e.dma_start(out=xbuf[:, j * D:(j + 1) * D],
                            in_=x3_in[k * 128:(k + 1) * 128, :]
                            ).then_inc(qh, 16)
            P.y(ldx)
            dxw = (qx[j].h, qx[j].inc(16))

            def zh(e, k=k, j=j, ti=ti, dxw=dxw, lnD=lnD, tv=list(tpD_v),
                   ov=list(oev_all)):
                e.wait_ge(dxw[0], dxw[1])
                e.wait_ge(v.h, lnD)
                if ti >= 2:
                    e.wait_ge(v.h, tv[-3])
                elif ov:
                    e.wait_ge(v.h, ov[-1])
                e.activation(zstage[:, j * D:(j + 1) * D],
                             xbuf[:, j * D:(j + 1) * D],
                             AF.Identity, bias=nbias[:, k:k + 1],
                             scale=rstd[:, k:k + 1]).then_inc(sc.h, 1)
            P.s(zh)
            zh_chunk.append(sc.inc())

            for grp in range(2):
                def tp(e, j=j, ti=ti, grp=grp, zs=zh_chunk[-1], c0=c0_q,
                       tv=list(tpD_v)):
                    e.wait_ge(sc.h, zs)
                    if ti == 0 and grp == 0:
                        e.wait_ge(qmi.h, c0)
                    if len(tv) >= 1:
                        e.wait_ge(v.h, tv[-1])
                    for jj in range(8):
                        dt = grp * 8 + jj
                        ins = e.transpose(
                            ps_tp[:, jj * 128:(jj + 1) * 128],
                            zstage[:, j * D + dt * 128:j * D + (dt + 1) * 128],
                            id_sb[:, :])
                        if jj == 7:
                            ins.then_inc(t.h, 1)
                P.t(tp)
                tval = t.inc()

                def ev(e, ti=ti, grp=grp, tval=tval):
                    e.wait_ge(t.h, tval)
                    for jj in range(8):
                        dt = grp * 8 + jj
                        ins = e.tensor_copy(
                            zt[:, dt * 512 + ti * 128:dt * 512 + (ti + 1) * 128],
                            ps_tp[:, jj * 128:(jj + 1) * 128])
                        if jj == 7:
                            ins.then_inc(v.h, 1)
                P.v(ev)
                tpD_v.append(v.inc())
        zt_v = tpD_v[-1]
        last_zh_s[0] = zh_chunk[-1]

        hd_chunk = []
        sil_chunk = []
        for it in range(NI):
            par = it % 2

            def ldw(e, it=it, par=par, wbg=wb_g, gv=list(guev),
                    gh=gwg[par].h):
                if it < 2 and not gv:
                    e.wait_ge(gmw.h, wbg)
                if len(gv) >= 2:
                    e.wait_ge(t.h, gv[-2])
                e.dma_start(out=wg[:, par * D:(par + 1) * D],
                            in_=gate_b[it]).then_inc(gh, 16)
                e.dma_start(out=wu[:, par * D:(par + 1) * D],
                            in_=up_b[it]).then_inc(gh, 16)
            P.g(ldw)
            ww = (gwg[par].h, gwg[par].inc(32))

            def gmmG(e, it=it, par=par, ww=ww, ztv=zt_v, sl=list(sil_chunk),
                     oa=list(oev_all)):
                e.wait_ge(ww[0], ww[1])
                e.wait_ge(v.h, ztv)
                if sl:
                    e.wait_ge(sc.h, sl[-1])
                for dt in range(ND):
                    ins = e.matmul(
                        ps_g[:, :],
                        wg[:, par * D + dt * 128:par * D + (dt + 1) * 128],
                        zt[:, dt * 512:(dt + 1) * 512],
                        start=(dt == 0), stop=(dt == ND - 1))
                    if dt == ND - 1:
                        ins.then_inc(t.h, 1)
            P.t(gmmG)
            gvG = t.inc()

            def gmmU(e, it=it, par=par, ww=ww, ztv=zt_v, hv=list(hd_chunk),
                     oa=list(oev_all)):
                if hv:
                    e.wait_ge(v.h, hv[-1])
                elif oa:
                    e.wait_ge(v.h, oa[-1])
                for dt in range(ND):
                    ins = e.matmul(
                        ps_u[:, :],
                        wu[:, par * D + dt * 128:par * D + (dt + 1) * 128],
                        zt[:, dt * 512:(dt + 1) * 512],
                        start=(dt == 0), stop=(dt == ND - 1))
                    if dt == ND - 1:
                        ins.then_inc(t.h, 1)
            P.t(gmmU)
            guev.append(t.inc())

            def sil(e, it=it, gv=gvG):
                e.wait_ge(t.h, gv)
                e.activation(sgb[:, :], ps_g[:, :], AF.Silu,
                             bias=0.0, scale=gsc[:, it:it + 1]).then_inc(sc.h, 1)
            P.s(sil)
            silv = sc.inc()
            sil_chunk.append(silv)

            def hdw(e, it=it, silv=silv, uv=guev[-1]):
                e.wait_ge(sc.h, silv)
                e.wait_ge(t.h, uv)
                e.scalar_tensor_tensor(hd[:, it * 512:(it + 1) * 512],
                                       ps_u[:, :], usc[:, it:it + 1],
                                       sgb[:, :], op0=ALU.mult,
                                       op1=ALU.mult).then_inc(v.h, 1)
            P.v(hdw)
            hd_chunk.append(v.inc())
        hd_ready = hd_chunk[-1]

        oev_chunk = []
        for p_ in range(2):
            for hh in range(2):
                for it in range(NI):
                    par = it % 4

                    def ldd(e, it=it, par=par, hh=hh, wbg=wb_g,
                            ot=list(omm_t), gh=gwd[par].h):
                        if not ot and it < 4:
                            e.wait_ge(gmw.h, wbg)
                        if len(ot) >= 4:
                            e.wait_ge(t.h, ot[-4])
                        e.dma_start(
                            out=wd[:, par * 1024:(par + 1) * 1024],
                            in_=down_b[it * 128:(it + 1) * 128,
                                       hh * 1024:(hh + 1) * 1024]
                            ).then_inc(gh, 16)
                    P.g(ldd)
                    dww = (gwd[par].h, gwd[par].inc(16))

                    def omm(e, it=it, par=par, p_=p_, hh=hh, dww=dww,
                            hr=hd_ready, dsr=ds_ready, oc=list(oev_chunk),
                            oa=list(oev_all), la=last_add_v[0]):
                        e.wait_ge(dww[0], dww[1])
                        if it == 0:
                            e.wait_ge(v.h, hr)
                            e.wait_ge(v.h, dsr)
                            if oc:
                                e.wait_ge(v.h, oc[-1])
                            elif oa:
                                e.wait_ge(v.h, oa[-1])
                        for sub in range(4):
                            tt = 2 * p_ + sub // 2
                            hc = 2 * hh + sub % 2
                            ins = e.matmul(
                                ps_o[sub][:, :],
                                hd[:, it * 512 + tt * 128:it * 512 + tt * 128 + 128],
                                wd[:, par * 1024 + (hc % 2) * 512:
                                   par * 1024 + (hc % 2) * 512 + 512],
                                start=(it == 0), stop=(it == NI - 1))
                            if sub == 3:
                                ins.then_inc(t.h, 1)
                    P.t(omm)
                    omm_t.append(t.inc())

                def oev(e, p_=p_, hh=hh, otv=omm_t[-1], dsr=ds_ready,
                        la=last_add_v[0]):
                    e.wait_ge(t.h, otv)
                    if la is not None:
                        e.wait_ge(v.h, la)
                    for sub in range(4):
                        tt = 2 * p_ + sub // 2
                        hc = 2 * hh + sub % 2
                        ins = e.tensor_tensor(
                            och[:, tt * D + hc * 512:tt * D + (hc + 1) * 512],
                            ps_o[sub][:, :],
                            dsrep[:, hc * 512:(hc + 1) * 512], op=ALU.mult)
                        if sub == 3:
                            ins.then_inc(v.h, 1)
                P.v(oev)
                oev_chunk.append(v.inc())
                oev_all.append(oev_chunk[-1])

        cst_v, cst_s = [], []
        for ti in range(4):
            def ost_v(e, ti=ti, ov=oev_chunk[-1]):
                e.wait_ge(v.h, ov)
                e.reduce_sum(s1b[:, ti:ti + 1], och[:, ti * D:(ti + 1) * D],
                             axis=AX.X).then_inc(v.h, 1)
            P.v(ost_v)
            cst_v.append(v.inc())

            def ost_s(e, ti=ti, ov=oev_chunk[-1]):
                e.wait_ge(v.h, ov)
                e.activation(trash[:, :], och[:, ti * D:(ti + 1) * D],
                             AF.Square, accum_out=s2b[:, ti:ti + 1]
                             ).then_inc(sc.h, 1)
            P.s(ost_s)
            cst_s.append(sc.inc())

        lnO = emit_ln_finalize(P, S, s1b, s2b, rstdb, nbiasb, tmpb, D, EPS, 4,
                               wait=[(v.h, cst_v[-1]), (sc.h, cst_s[-1])])

        addD_v = []
        for ti in range(4):
            k = tc * 4 + ti
            j = ti % 2

            def ldx2(e, k=k, j=j, lnO=lnO, sq=list(st4_q), qh=qx[j].h,
                     sh=qst[j].h):
                e.wait_ge(v.h, lnO)
                if len(sq) >= 2:
                    e.wait_ge(sh, sq[-2])
                e.dma_start(out=xbuf[:, j * D:(j + 1) * D],
                            in_=x3_in[k * 128:(k + 1) * 128, :]
                            ).then_inc(qh, 16)
            P.y(ldx2)
            dxw = (qx[j].h, qx[j].inc(16))

            def nrmO(e, ti=ti, lnO=lnO, av=list(addD_v)):
                e.wait_ge(v.h, lnO)
                if av:
                    e.wait_ge(v.h, av[-1])
                e.activation(trash[:, :], och[:, ti * D:(ti + 1) * D],
                             AF.Identity, bias=nbiasb[:, ti:ti + 1],
                             scale=rstdb[:, ti:ti + 1]).then_inc(sc.h, 1)
            P.s(nrmO)
            nov = sc.inc()

            def addD(e, j=j, nov=nov, dxw=dxw):
                e.wait_ge(sc.h, nov)
                e.wait_ge(dxw[0], dxw[1])
                e.tensor_tensor(xbuf[:, j * D:(j + 1) * D],
                                xbuf[:, j * D:(j + 1) * D], trash[:, :],
                                op=ALU.add).then_inc(v.h, 1)
            P.v(addD)
            addD_v.append(v.inc())

            def stx(e, k=k, j=j, av=addD_v[-1], sh=qst[j].h):
                e.wait_ge(v.h, av)
                e.dma_start(out=x4_out[k * 128:(k + 1) * 128, :],
                            in_=xbuf[:, j * D:(j + 1) * D]).then_inc(sh, 16)
            P.y(stx)
            st4_q.append(qst[j].inc(16))
        last_add_v[0] = addD_v[-1]

    def finD(e, l0=qst[0].n, l1=qst[1].n):
        e.wait_ge(qst[0].h, l0)
        e.wait_ge(qst[1].h, l1)
    P.y(finD)


def build_d_program():
    nc = bass.Bass("TRN2", target_bir_lowering=False)
    x3_in = nc.dram_tensor("x3", [T, D], F32, kind="ExternalInput").ap()
    gate_t = nc.dram_tensor("gate_t", [NI, 128, D], F32,
                            kind="ExternalInput").ap()
    up_t = nc.dram_tensor("up_t", [NI, 128, D], F32, kind="ExternalInput").ap()
    down_t = nc.dram_tensor("down_t", [INTER, D], F32,
                            kind="ExternalInput").ap()
    gs_in = nc.dram_tensor("gs", [128, NI], F32, kind="ExternalInput").ap()
    us_in = nc.dram_tensor("us", [128, NI], F32, kind="ExternalInput").ap()
    ds_in = nc.dram_tensor("ds", [D], F32, kind="ExternalInput").ap()
    ident_in = nc.dram_tensor("ident", [128, 128], BF16,
                              kind="ExternalInput").ap()
    x4_out = nc.dram_tensor("x4", [T, D], F32, kind="ExternalOutput").ap()
    gate_b = nc.dram_tensor("gate_b", [NI, 128, D], BF16, kind="Internal").ap()
    up_b = nc.dram_tensor("up_b", [NI, 128, D], BF16, kind="Internal").ap()
    down_b = nc.dram_tensor("down_b", [INTER, D], BF16, kind="Internal").ap()
    with contextlib.ExitStack() as ctx:
        S = mk_sems(nc, ctx)
        P = Prog(nc)
        build_phase_d(nc, ctx, P, S, x3_in, gate_t, up_t, down_t, gs_in,
                      us_in, ds_in, x4_out, ident_in, gate_b, up_b, down_b)
        P.emit()
    return nc



def emit_barrier(P, S):
    cur = {k: S[k].n for k in S}

    def mk(me):
        def go(e, cur=dict(cur)):
            for k, ct in S.items():
                if cur[k] > 0:
                    e.wait_ge(ct.h, cur[k])
        return go
    P.t(mk('t'))
    P.v(mk('v'))
    P.s(mk('s'))
    P.g(mk('g'))
    P.y(mk('q'))


def build_full_program():
    nc = bass.Bass("TRN2", target_bir_lowering=False)
    x_in = nc.dram_tensor("x", [T, D], F32, kind="ExternalInput").ap()
    facs = nc.dram_tensor("factors", [8, D, D], F32, kind="ExternalInput").ap()
    fbias = nc.dram_tensor("fbias", [D], F32, kind="ExternalInput").ap()
    wqt = nc.dram_tensor("wqt", [D, D], F32, kind="ExternalInput").ap()
    wkt = nc.dram_tensor("wkt", [D, 128], F32, kind="ExternalInput").ap()
    wvt = nc.dram_tensor("wvt", [D, 128], F32, kind="ExternalInput").ap()
    wot = nc.dram_tensor("wot", [D, D], F32, kind="ExternalInput").ap()
    nsc = nc.dram_tensor("nscol", [128, ND], F32, kind="ExternalInput").ap()
    ssc = nc.dram_tensor("sscol", [128, ND], F32, kind="ExternalInput").ap()
    gate_t = nc.dram_tensor("gate_t", [NI, 128, D], F32,
                            kind="ExternalInput").ap()
    up_t = nc.dram_tensor("up_t", [NI, 128, D], F32, kind="ExternalInput").ap()
    down_t = nc.dram_tensor("down_t", [INTER, D], F32,
                            kind="ExternalInput").ap()
    gs_in = nc.dram_tensor("gs", [128, NI], F32, kind="ExternalInput").ap()
    us_in = nc.dram_tensor("us", [128, NI], F32, kind="ExternalInput").ap()
    ds_in = nc.dram_tensor("ds", [D], F32, kind="ExternalInput").ap()
    ident_in = nc.dram_tensor("ident", [128, 128], BF16,
                              kind="ExternalInput").ap()
    x4_out = nc.dram_tensor("x4", [T, D], F32, kind="ExternalOutput").ap()

    c_scr = nc.dram_tensor("c_scr", [T, D], BF16, kind="Internal").ap()
    fscr = nc.dram_tensor("fscr", [D, D], BF16, kind="Internal").ap()
    x1_scr = nc.dram_tensor("x1_scr", [T, D], F32, kind="Internal").ap()
    x2_scr = nc.dram_tensor("x2_scr", [T, D], F32, kind="Internal").ap()
    x3_scr = nc.dram_tensor("x3_scr", [T, D], F32, kind="Internal").ap()
    gate_b = nc.dram_tensor("gate_b", [NI, 128, D], BF16, kind="Internal").ap()
    up_b = nc.dram_tensor("up_b", [NI, 128, D], BF16, kind="Internal").ap()
    down_b = nc.dram_tensor("down_b", [INTER, D], BF16, kind="Internal").ap()

    with contextlib.ExitStack() as sems_ctx:
        S = mk_sems(nc, sems_ctx)
        P = Prog(nc)
        gmw = S['gmw']

        with contextlib.ExitStack() as ctx:
            fbuf = ctx.enter_context(nc.sbuf_tensor("fbuf", [128, 3 * D],
                                                    F32))
            fstate = {}

            def factor_cb(k):
                emit_factor_group(P, S, facs, fbuf, fscr, k, fstate)

            build_phase_a(nc, ctx, P, S, x_in, x1_scr, ident_in, c_scr,
                          factor_cb=factor_cb)
            emit_barrier(P, S)
        fsum_wait = (S['gfc0'].h, S['gfc0'].n)
        with contextlib.ExitStack() as ctx:
            build_phase_b(nc, ctx, P, S, x1_scr, facs, fbias, x2_scr, ident_in,
                          fsum_pre=(fscr, fsum_wait))
            def wcast(e):
                e.dma_start(out=gate_b, in_=gate_t).then_inc(gmw.h, 16)
                e.dma_start(out=up_b, in_=up_t).then_inc(gmw.h, 16)
                e.dma_start(out=down_b, in_=down_t).then_inc(gmw.h, 16)
            P.g(wcast)
            wb_g = gmw.inc(48)
            emit_barrier(P, S)
        with contextlib.ExitStack() as ctx:
            build_phase_c(nc, ctx, P, S, x2_scr, wqt, wkt, wvt, wot, nsc, ssc,
                          x3_scr, ident_in)
            emit_barrier(P, S)
        with contextlib.ExitStack() as ctx:
            build_phase_d(nc, ctx, P, S, x3_scr, gate_t, up_t, down_t, gs_in,
                          us_in, ds_in, x4_out, ident_in, gate_b, up_b,
                          down_b, wcast_done=wb_g)
            P.emit()
    return nc


# version 09
# speedup vs baseline: 1.0605x; 1.0605x over previous
import contextlib
import numpy as np
import ml_dtypes
import concourse.bass as bass
import concourse.mybir as mybir
from concourse.bass_utils import run_bass_kernel_spmd

F32 = mybir.dt.float32
BF16 = mybir.dt.bfloat16
AF = mybir.ActivationFunctionType
ALU = mybir.AluOpType
AX = mybir.AxisListType

T = 2048
D = 2048
NT = T // 128
ND = D // 128
EPS = 1e-5
NCORES = 8


class Ct:

    def __init__(self, handle):
        self.h = handle
        self.n = 0

    def inc(self, k=1):
        self.n += k
        return self.n


class Prog:

    ENGS = ("t", "v", "s", "g", "y")

    def __init__(self, nc):
        self.nc = nc
        self.qs = {k: [] for k in self.ENGS}

    def t(self, f):
        self.qs["t"].append(f)

    def v(self, f):
        self.qs["v"].append(f)

    def s(self, f):
        self.qs["s"].append(f)

    def g(self, f):
        self.qs["g"].append(f)

    def y(self, f):
        self.qs["y"].append(f)

    def emit(self):
        def runner(fs):
            def go(e):
                for f in fs:
                    f(e)
            return go
        with self.nc.Block() as blk:
            if self.qs["t"]:
                blk.tensor(runner(self.qs["t"]))
            if self.qs["v"]:
                blk.vector(runner(self.qs["v"]))
            if self.qs["s"]:
                blk.scalar(runner(self.qs["s"]))
            if self.qs["g"]:
                blk.gpsimd(runner(self.qs["g"]))
            if self.qs["y"]:
                blk.sync(runner(self.qs["y"]))
        self.qs = {k: [] for k in self.ENGS}


SEM_KEYS = [
    "t", "v", "s",
    "qmi",
    "qx0", "qx1",
    "qst0", "qst1",
    "qcr",
    "qcs",
    "gfc0", "gfc1", "gfc2", "gfc3",
    "gms",
    "gkv",
    "gwq",
    "gwo",
    "gmw",
    "gwg0", "gwg1", "gwg2", "gwg3",
    "gwd0", "gwd1", "gwd2", "gwd3",
]


def mk_sems(nc, ctx):
    return {k: Ct(ctx.enter_context(nc.semaphore("sem_" + k)))
            for k in SEM_KEYS}



def emit_ln_finalize(P, S, s1, s2, rstd, nbias, tmp, n_elem, eps, ncols,
                     wait):
    inv = 1.0 / n_elem
    v, sc = S['v'], S['s']
    m = tmp[:, 0:ncols]
    ex2 = tmp[:, ncols:2 * ncols]
    mm = tmp[:, 2 * ncols:3 * ncols]

    def v1(e, wait=tuple(wait)):
        for h, val in wait:
            e.wait_ge(h, val)
        e.tensor_scalar_mul(m, s1[:, 0:ncols], inv)
        e.tensor_scalar_mul(ex2, s2[:, 0:ncols], inv).then_inc(v.h, 1)
    P.v(v1)
    w1 = v.inc()

    def v2(e, w1=w1):
        e.wait_ge(v.h, w1)
        e.tensor_mul(mm, m, m).then_inc(v.h, 1)
    P.v(v2)
    w2 = v.inc()

    def v3(e, w2=w2):
        e.wait_ge(v.h, w2)
        e.tensor_sub(ex2, ex2, mm).then_inc(v.h, 1)
    P.v(v3)
    w3 = v.inc()

    def v3b(e, w3=w3, eps=float(eps)):
        e.wait_ge(v.h, w3)
        e.tensor_scalar_add(ex2, ex2, eps).then_inc(v.h, 1)
    P.v(v3b)
    w3b = v.inc()

    def sf(e, w3b=w3b):
        e.wait_ge(v.h, w3b)
        e.activation(rstd[:, 0:ncols], ex2, AF.Sqrt, bias=0.0,
                     scale=1.0).then_inc(sc.h, 1)
    P.s(sf)
    ws = sc.inc()

    def v4(e, ws=ws):
        e.wait_ge(sc.h, ws)
        e.reciprocal(rstd[:, 0:ncols], rstd[:, 0:ncols]).then_inc(v.h, 1)
    P.v(v4)
    w4 = v.inc()

    def v5(e, w4=w4):
        e.wait_ge(v.h, w4)
        e.scalar_tensor_tensor(nbias[:, 0:ncols], m, -1.0, rstd[:, 0:ncols],
                               op0=ALU.mult, op1=ALU.mult).then_inc(v.h, 1)
    P.v(v5)
    return v.inc()


def emit_stats_tile(P, S, src_ap, k, s1, s2, trash, dma_wait):
    v, sc = S['v'], S['s']

    def st_v(e, src_ap=src_ap, k=k, dw=dma_wait):
        if dw is not None:
            e.wait_ge(dw[0], dw[1])
        e.reduce_sum(s1[:, k:k + 1], src_ap, axis=AX.X).then_inc(v.h, 1)
    P.v(st_v)
    vv = v.inc()

    def st_s(e, src_ap=src_ap, k=k, dw=dma_wait):
        if dw is not None:
            e.wait_ge(dw[0], dw[1])
        e.activation(trash, src_ap, AF.Square,
                     accum_out=s2[:, k:k + 1]).then_inc(sc.h, 1)
    P.s(st_s)
    sv = sc.inc()
    return vv, sv


def emit_factor_group(P, S, factors_in, fbuf, fscr, dt, fstate):
    v = S['v']
    gf = S['gfc0']
    acc = fbuf[:, 2 * D:3 * D]

    def ld0(e, dt=dt, pc=fstate.get('cast'), gh=gf.h):
        if pc is not None:
            e.wait_ge(gh, pc)
        e.dma_start(out=acc, in_=factors_in[0, dt * 128:(dt + 1) * 128, :]
                    ).then_inc(gh, 16)
    P.g(ld0)
    acc_ld = gf.inc(16)

    gadd = S['gms']
    add_vals = []
    for r in range(1, 8):
        sl = (r - 1) % 2

        def ldr(e, dt=dt, r=r, sl=sl, av=list(add_vals), gh=gf.h):
            if r >= 3:
                e.wait_ge(gadd.h, av[r - 3])
            e.dma_start(out=fbuf[:, sl * D:(sl + 1) * D],
                        in_=factors_in[r, dt * 128:(dt + 1) * 128, :]
                        ).then_inc(gh, 16)
        P.g(ldr)
        ldv = gf.inc(16)

        def add(e, r=r, sl=sl, ldv=ldv, al=acc_ld, av=list(add_vals),
                gh=gf.h):
            e.wait_ge(gh, ldv)
            if r == 1:
                e.wait_ge(gh, al)
            else:
                e.wait_ge(gadd.h, av[-1])
            e.tensor_add(acc, acc, fbuf[:, sl * D:(sl + 1) * D]
                         ).then_inc(gadd.h, 1)
        P.g(add)
        add_vals.append(gadd.inc())

    def cast(e, dt=dt, av=add_vals[-1], gh=gf.h):
        e.wait_ge(gadd.h, av)
        e.dma_start(out=fscr[dt * 128:(dt + 1) * 128, :], in_=acc
                    ).then_inc(gh, 16)
    P.g(cast)
    fstate['cast'] = gf.inc(16)




def build_phase_a(nc, ctx, P, S, x_in, x1_out, ident_in, c_scratch,
                  dbg=None, factor_cb=None):
    t, v, sc = S['t'], S['v'], S['s']
    qmi = S['qmi']
    qx = [S['qx0'], S['qx1']]
    qst = [S['qst0'], S['qst1']]
    qcr, qcs = S['qcr'], S['qcs']

    id_sb = ctx.enter_context(nc.sbuf_tensor("id_sb", [128, 128], BF16))
    xbuf = ctx.enter_context(nc.sbuf_tensor("xbuf", [128, 2 * D], F32))
    trash = ctx.enter_context(nc.sbuf_tensor("trash", [128, D], BF16))
    s1 = ctx.enter_context(nc.sbuf_tensor("s1", [128, NT], F32))
    s2 = ctx.enter_context(nc.sbuf_tensor("s2", [128, NT], F32))
    rstd = ctx.enter_context(nc.sbuf_tensor("rstd", [128, NT], F32))
    nbias = ctx.enter_context(nc.sbuf_tensor("nbias", [128, NT], F32))
    tmp = ctx.enter_context(nc.sbuf_tensor("tmp", [128, 3 * NT], F32))
    xn = ctx.enter_context(nc.sbuf_tensor("xn", [128, NT * D], BF16))
    xnt = ctx.enter_context(nc.sbuf_tensor("xnt", [128, ND * T], BF16))
    wc = ctx.enter_context(nc.sbuf_tensor("wc", [128, 2 * D], BF16))
    wct = ctx.enter_context(nc.sbuf_tensor("wct", [128, ND * 512], BF16))
    cch = ctx.enter_context(nc.sbuf_tensor("cch", [128, 4 * 512], BF16))
    smax = ctx.enter_context(nc.sbuf_tensor("smax", [128, 4], F32))
    srs = ctx.enter_context(nc.sbuf_tensor("srs", [128, 8], F32))

    ps_s = ctx.enter_context(nc.psum_tensor("ps_s", [128, 2048], F32))
    ps_t0 = ctx.enter_context(nc.psum_tensor("ps_t0", [128, 1024], BF16))
    ps_t1 = ctx.enter_context(nc.psum_tensor("ps_t1", [128, 1024], BF16))
    ps_tt = [ps_t0, ps_t1]
    ps_c0 = ctx.enter_context(nc.psum_tensor("ps_c0", [128, 512], F32))
    ps_c1 = ctx.enter_context(nc.psum_tensor("ps_c1", [128, 512], F32))
    ps_c = [ps_c0, ps_c1]

    def ld_id(e):
        e.dma_start(out=id_sb[:, :], in_=ident_in).then_inc(qmi.h, 16)
    P.y(ld_id)
    id_ready = qmi.inc(16)

    stat_v, stat_s = [], []
    for k in range(NT):
        j = k % 2

        def ld(e, k=k, j=j, sv=list(stat_v), ss=list(stat_s), qh=qx[j].h):
            if k >= 2:
                e.wait_ge(v.h, sv[k - 2])
                e.wait_ge(sc.h, ss[k - 2])
            e.dma_start(out=xbuf[:, j * D:(j + 1) * D],
                        in_=x_in[k * 128:(k + 1) * 128, :]).then_inc(qh, 16)
        P.y(ld)
        dw = (qx[j].h, qx[j].inc(16))
        vv, sv_ = emit_stats_tile(P, S, xbuf[:, j * D:(j + 1) * D], k,
                                  s1, s2, trash[:, :], dw)
        stat_v.append(vv)
        stat_s.append(sv_)

    ln_ready = emit_ln_finalize(P, S, s1, s2, rstd, nbias, tmp, D, EPS, NT,
                                wait=[(v.h, stat_v[-1]), (sc.h, stat_s[-1])])

    norm_s = []
    tpos_v = []
    for k in range(NT):
        j = k % 2

        def ld(e, k=k, j=j, ns=list(norm_s), qh=qx[j].h,
               pv=stat_v[-1], ps=stat_s[-1]):
            if k >= 2:
                e.wait_ge(sc.h, ns[k - 2])
            else:
                e.wait_ge(v.h, pv)
                e.wait_ge(sc.h, ps)
            e.dma_start(out=xbuf[:, j * D:(j + 1) * D],
                        in_=x_in[k * 128:(k + 1) * 128, :]).then_inc(qh, 16)
        P.y(ld)
        dw = (qx[j].h, qx[j].inc(16))

        def nrm(e, k=k, j=j, dw=dw, lnr=ln_ready):
            e.wait_ge(dw[0], dw[1])
            e.wait_ge(v.h, lnr)
            e.activation(xn[:, k * D:(k + 1) * D], xbuf[:, j * D:(j + 1) * D],
                         AF.Identity, bias=nbias[:, k:k + 1],
                         scale=rstd[:, k:k + 1]).then_inc(sc.h, 1)
        P.s(nrm)
        norm_s.append(sc.inc())

        for grp in range(2):
            pst = ps_tt[grp]

            def tp(e, k=k, grp=grp, pst=pst, nsv=norm_s[-1], idr=id_ready,
                   tv=list(tpos_v)):
                e.wait_ge(sc.h, nsv)
                if k == 0 and grp == 0:
                    e.wait_ge(qmi.h, idr)
                if len(tv) >= 2:
                    e.wait_ge(v.h, tv[-2])
                for jj in range(8):
                    jt = grp * 8 + jj
                    ins = e.transpose(
                        pst[:, jj * 128:(jj + 1) * 128],
                        xn[:, k * D + jt * 128:k * D + (jt + 1) * 128],
                        id_sb[:, :])
                    if jj == 7:
                        ins.then_inc(t.h, 1)
            P.t(tp)
            tval = t.inc()

            def ev(e, k=k, grp=grp, pst=pst, tval=tval):
                e.wait_ge(t.h, tval)
                for jj in range(8):
                    jt = grp * 8 + jj
                    ins = e.tensor_copy(
                        xnt[:, jt * T + k * 128:jt * T + (k + 1) * 128],
                        pst[:, jj * 128:(jj + 1) * 128])
                    if jj == 7:
                        ins.then_inc(v.h, 1)
            P.v(ev)
            tpos_v.append(v.inc())
        if factor_cb is not None:
            factor_cb(k)

    xn_all_s = norm_s[-1]
    xnt_all_v = tpos_v[-1]

    if dbg is not None:
        xn_dbg, xnt_dbg = dbg

        def dump(e, xa=xn_all_s, xv=xnt_all_v, qh=S['qcr'].h):
            e.wait_ge(sc.h, xa)
            e.wait_ge(v.h, xv)
            e.dma_start(out=xn_dbg, in_=xn[:, :]).then_inc(qh, 16)
            e.dma_start(out=xnt_dbg, in_=xnt[:, :]).then_inc(qh, 16)
        P.y(dump)
        S['qcr'].inc(32)

    exp_done = []
    cst_done = []
    cev_v = []
    wt_v = []
    for ic in range(4):
        for il in range(4):
            it = ic * 4 + il
            wpar = il % 2

            def smm(e, it=it, xa=xn_all_s, ed=list(exp_done)):
                e.wait_ge(sc.h, xa)
                if ed:
                    e.wait_ge(sc.h, ed[-1])
                for jt in range(4):
                    for ct in range(NT):
                        ins = e.matmul(
                            ps_s[:, jt * 512:(jt + 1) * 512],
                            xn[:, ct * D + it * 128:ct * D + it * 128 + 128],
                            xn[:, ct * D + jt * 512:ct * D + (jt + 1) * 512],
                            start=(ct == 0), stop=(ct == NT - 1))
                        if jt == 3 and ct == NT - 1:
                            ins.then_inc(t.h, 1)
            P.t(smm)
            smm_t = t.inc()

            def mx(e, il=il, smm_t=smm_t):
                e.wait_ge(t.h, smm_t)
                e.reduce_max(smax[:, il:il + 1], ps_s[:, :], axis=AX.X,
                             negate=True).then_inc(v.h, 1)
            P.v(mx)
            mx_v = v.inc()

            def ex(e, il=il, wpar=wpar, mx_v=mx_v, wv=list(wt_v)):
                e.wait_ge(v.h, mx_v)
                if len(wv) >= 4:
                    e.wait_ge(v.h, wv[-4])
                e.activation(wc[:, wpar * D:(wpar + 1) * D], ps_s[:, :],
                             AF.Exp, bias=smax[:, il:il + 1], scale=1.0,
                             accum_out=srs[:, il:il + 1]).then_inc(sc.h, 1)
            P.s(ex)
            exp_done.append(sc.inc())

            def rcp(e, il=il, ev_=exp_done[-1]):
                e.wait_ge(sc.h, ev_)
                e.reciprocal(srs[:, 4 + il:5 + il],
                             srs[:, il:il + 1]).then_inc(v.h, 1)
            P.v(rcp)
            rcp_v = v.inc()

            def nrmw(e, wpar=wpar, il=il, rcp_v=rcp_v):
                e.wait_ge(v.h, rcp_v)
                e.tensor_scalar_mul(wc[:, wpar * D:(wpar + 1) * D],
                                    wc[:, wpar * D:(wpar + 1) * D],
                                    srs[:, 4 + il:5 + il]).then_inc(v.h, 1)
            P.v(nrmw)
            wn_v = v.inc()

            for grp in range(2):
                pst = ps_tt[grp]

                def tpw(e, il=il, wpar=wpar, grp=grp, pst=pst, wn=wn_v,
                        wv=list(wt_v), tv=xnt_all_v):
                    e.wait_ge(v.h, wn)
                    if len(wv) >= 2:
                        e.wait_ge(v.h, wv[-2])
                    else:
                        e.wait_ge(v.h, tv)
                    for jj in range(8):
                        jt = grp * 8 + jj
                        ins = e.transpose(
                            pst[:, jj * 128:(jj + 1) * 128],
                            wc[:, wpar * D + jt * 128:wpar * D + (jt + 1) * 128],
                            id_sb[:, :])
                        if jj == 7:
                            ins.then_inc(t.h, 1)
                P.t(tpw)
                tval = t.inc()

                def evw(e, il=il, grp=grp, pst=pst, tval=tval):
                    e.wait_ge(t.h, tval)
                    for jj in range(8):
                        jt = grp * 8 + jj
                        ins = e.tensor_copy(
                            wct[:, jt * 512 + il * 128:jt * 512 + (il + 1) * 128],
                            pst[:, jj * 128:(jj + 1) * 128])
                        if jj == 7:
                            ins.then_inc(v.h, 1)
                P.v(evw)
                wt_v.append(v.inc())
        wct_ready = wt_v[-1]

        for ct in range(NT):
            pc = ps_c[ct % 2]
            slot = ct % 4

            def cmm(e, ct=ct, pc=pc, wr=wct_ready, cv=list(cev_v)):
                e.wait_ge(v.h, wr)
                if len(cv) >= 2:
                    e.wait_ge(v.h, cv[-2])
                for jt in range(ND):
                    ins = e.matmul(
                        pc[:, :],
                        xnt[:, jt * T + ct * 128:jt * T + (ct + 1) * 128],
                        wct[:, jt * 512:(jt + 1) * 512],
                        start=(jt == 0), stop=(jt == ND - 1))
                    if jt == ND - 1:
                        ins.then_inc(t.h, 1)
            P.t(cmm)
            cval = t.inc()

            def cev(e, slot=slot, pc=pc, cval=cval, cs=list(cst_done),
                    qh=qcs.h):
                e.wait_ge(t.h, cval)
                if len(cs) >= 1 and slot == 0:
                    e.wait_ge(qh, cs[-1])
                e.tensor_copy(cch[:, slot * 512:(slot + 1) * 512],
                              pc[:, :]).then_inc(v.h, 1)
            P.v(cev)
            cev_v.append(v.inc())

            if slot == 3:
                def cst(e, ic=ic, ct=ct, cv=cev_v[-1]):
                    e.wait_ge(v.h, cv)
                    dst = c_scratch.rearrange("(ct p) i -> p ct i", p=128)
                    e.dma_start(
                        out=dst[:, ct - 3:ct + 1, ic * 512:(ic + 1) * 512],
                        in_=cch[:, :].rearrange("p (q i) -> p q i", i=512)
                        ).then_inc(qcs.h, 16)
                P.y(cst)
                cst_done.append(qcs.inc(16))

    crow = xn
    for k in range(NT):
        def ld(e, k=k, cs=cst_done[-1]):
            if k == 0:
                e.wait_ge(qcs.h, cs)
            e.dma_start(out=crow[:, k * D:(k + 1) * D],
                        in_=c_scratch[k * 128:(k + 1) * 128, :]
                        ).then_inc(qcr.h, 16)
        P.y(ld)
        qcr.inc(16)
    crow_all = (qcr.h, qcr.n)

    stat4_v, stat4_s = [], []
    for k in range(NT):
        vv, sv_ = emit_stats_tile(P, S, crow[:, k * D:(k + 1) * D], k,
                                  s1, s2, trash[:, :], crow_all)
        stat4_v.append(vv)
        stat4_s.append(sv_)

    ln4 = emit_ln_finalize(P, S, s1, s2, rstd, nbias, tmp, D, EPS, NT,
                           wait=[(v.h, stat4_v[-1]), (sc.h, stat4_s[-1])])

    st1_q = []
    add_v = []
    for k in range(NT):
        j = k % 2

        def ldx(e, k=k, j=j, sq=list(st1_q), qh=qx[j].h, sh=qst[j].h,
                ps=norm_s[-1]):
            if k >= 2:
                e.wait_ge(sh, sq[k - 2])
            else:
                e.wait_ge(sc.h, ps)
            e.dma_start(out=xbuf[:, j * D:(j + 1) * D],
                        in_=x_in[k * 128:(k + 1) * 128, :]).then_inc(qh, 16)
        P.y(ldx)
        dxw = (qx[j].h, qx[j].inc(16))

        def nrm4(e, k=k, ln4=ln4, av=list(add_v)):
            e.wait_ge(v.h, ln4)
            if av:
                e.wait_ge(v.h, av[-1])
            e.activation(trash[:, :], crow[:, k * D:(k + 1) * D], AF.Identity,
                         bias=nbias[:, k:k + 1],
                         scale=rstd[:, k:k + 1]).then_inc(sc.h, 1)
        P.s(nrm4)
        nv = sc.inc()

        def add4(e, k=k, j=j, nv=nv, dxw=dxw):
            e.wait_ge(sc.h, nv)
            e.wait_ge(dxw[0], dxw[1])
            e.tensor_tensor(xbuf[:, j * D:(j + 1) * D],
                            xbuf[:, j * D:(j + 1) * D], trash[:, :],
                            op=ALU.add).then_inc(v.h, 1)
        P.v(add4)
        add_v.append(v.inc())

        def stx(e, k=k, j=j, av=add_v[-1], sh=qst[j].h):
            e.wait_ge(v.h, av)
            e.dma_start(out=x1_out[k * 128:(k + 1) * 128, :],
                        in_=xbuf[:, j * D:(j + 1) * D]).then_inc(sh, 16)
        P.y(stx)
        st1_q.append(qst[j].inc(16))

    def fin(e, l0=qst[0].n, l1=qst[1].n):
        e.wait_ge(qst[0].h, l0)
        e.wait_ge(qst[1].h, l1)
    P.y(fin)



def _ident():
    return np.eye(128).astype(ml_dtypes.bfloat16)


def build_a_program():
    nc = bass.Bass("TRN2", target_bir_lowering=False)
    x_in = nc.dram_tensor("x", [T, D], F32, kind="ExternalInput").ap()
    ident_in = nc.dram_tensor("ident", [128, 128], BF16,
                              kind="ExternalInput").ap()
    x1_out = nc.dram_tensor("x1", [T, D], F32, kind="ExternalOutput").ap()
    c_scr = nc.dram_tensor("c_scr", [T, D], BF16, kind="Internal").ap()
    fscr = nc.dram_tensor("fscr", [D, D], BF16, kind="Internal").ap()
    with contextlib.ExitStack() as ctx:
        S = mk_sems(nc, ctx)
        P = Prog(nc)
        build_phase_a(nc, ctx, P, S, x_in, x1_out, ident_in, c_scr)
        P.emit()
    return nc


_PROGS = {}


def _get_programs():
    if "a" not in _PROGS:
        _PROGS["a"] = build_a_program()
        _PROGS["b"] = build_b_program()
        _PROGS["c"] = build_c_program()
        _PROGS["d"] = build_d_program()
    return _PROGS


def _tile_gateup(w):
    wt = w.T
    return np.ascontiguousarray(
        wt.reshape(16, 128, 44, 128).transpose(2, 1, 0, 3).reshape(44, 128, 2048))


FUSED = True


def _prep_host(L0_ffn_factors, L0_ffn_bias, L1_attn_norm_scale,
               L1_attn_split_scale, L1_attn_Wq, L1_attn_Wk, L1_attn_Wv,
               L1_attn_Wo, L1_ffn_gate_w, L1_ffn_up_w, L1_ffn_down_w,
               L1_ffn_gate_s, L1_ffn_up_s, L1_ffn_down_s):
    f32 = np.float32
    return {
        "factors": np.ascontiguousarray(np.asarray(L0_ffn_factors, f32)),
        "fbias": np.ascontiguousarray(np.asarray(L0_ffn_bias, f32)),
        "wqt": np.ascontiguousarray(np.asarray(L1_attn_Wq, f32).T),
        "wkt": np.ascontiguousarray(np.asarray(L1_attn_Wk, f32).T),
        "wvt": np.ascontiguousarray(np.asarray(L1_attn_Wv, f32).T),
        "wot": np.ascontiguousarray(np.asarray(L1_attn_Wo, f32).T),
        "nscol": np.ascontiguousarray(
            np.asarray(L1_attn_norm_scale, f32).reshape(16, 128).T),
        "sscol": np.ascontiguousarray(
            np.asarray(L1_attn_split_scale, f32).reshape(16, 128).T),
        "gate_t": _tile_gateup(np.asarray(L1_ffn_gate_w, f32)),
        "up_t": _tile_gateup(np.asarray(L1_ffn_up_w, f32)),
        "down_t": np.ascontiguousarray(np.asarray(L1_ffn_down_w, f32).T),
        "gs": np.ascontiguousarray(
            np.asarray(L1_ffn_gate_s, f32).reshape(44, 128).T),
        "us": np.ascontiguousarray(
            np.asarray(L1_ffn_up_s, f32).reshape(44, 128).T),
        "ds": np.ascontiguousarray(np.asarray(L1_ffn_down_s, f32)),
        "ident": _ident(),
    }


def kernel(L0_attn_x, L0_ffn_factors, L0_ffn_bias, L1_attn_norm_scale,
           L1_attn_split_scale, L1_attn_Wq, L1_attn_Wk, L1_attn_Wv,
           L1_attn_Wo, L1_ffn_gate_w, L1_ffn_up_w, L1_ffn_down_w,
           L1_ffn_gate_s, L1_ffn_up_s, L1_ffn_down_s):
    if FUSED:
        if "full" not in _PROGS:
            _PROGS["full"] = build_full_program()
        nc_full = _PROGS["full"]
        x = np.ascontiguousarray(np.asarray(L0_attn_x, dtype=np.float32))
        shared = _prep_host(L0_ffn_factors, L0_ffn_bias, L1_attn_norm_scale,
                            L1_attn_split_scale, L1_attn_Wq, L1_attn_Wk,
                            L1_attn_Wv, L1_attn_Wo, L1_ffn_gate_w,
                            L1_ffn_up_w, L1_ffn_down_w, L1_ffn_gate_s,
                            L1_ffn_up_s, L1_ffn_down_s)
        in_maps = [dict(shared, x=np.ascontiguousarray(x[b]))
                   for b in range(NCORES)]
        res = run_bass_kernel_spmd(nc_full, in_maps,
                                   core_ids=list(range(NCORES)))
        return np.stack([np.asarray(r["x4"]) for r in res.results]
                        ).astype(np.float32)
    progs = _get_programs()
    ident = _ident()
    cores = list(range(NCORES))
    x = np.ascontiguousarray(np.asarray(L0_attn_x, dtype=np.float32))

    in_maps = [{"x": np.ascontiguousarray(x[b]), "ident": ident}
               for b in range(NCORES)]
    res = run_bass_kernel_spmd(progs["a"], in_maps, core_ids=cores)
    x1 = [np.asarray(r["x1"]) for r in res.results]

    facs = np.ascontiguousarray(np.asarray(L0_ffn_factors, dtype=np.float32))
    fbias = np.ascontiguousarray(np.asarray(L0_ffn_bias, dtype=np.float32))
    in_maps = [{"x1": x1[b], "factors": facs, "fbias": fbias, "ident": ident}
               for b in range(NCORES)]
    res = run_bass_kernel_spmd(progs["b"], in_maps, core_ids=cores)
    x2 = [np.asarray(r["x2"]) for r in res.results]

    wqt = np.ascontiguousarray(np.asarray(L1_attn_Wq, dtype=np.float32).T)
    wkt = np.ascontiguousarray(np.asarray(L1_attn_Wk, dtype=np.float32).T)
    wvt = np.ascontiguousarray(np.asarray(L1_attn_Wv, dtype=np.float32).T)
    wot = np.ascontiguousarray(np.asarray(L1_attn_Wo, dtype=np.float32).T)
    nscol = np.ascontiguousarray(
        np.asarray(L1_attn_norm_scale, dtype=np.float32).reshape(16, 128).T)
    sscol = np.ascontiguousarray(
        np.asarray(L1_attn_split_scale, dtype=np.float32).reshape(16, 128).T)
    in_maps = [{"x2": x2[b], "wqt": wqt, "wkt": wkt, "wvt": wvt, "wot": wot,
                "nscol": nscol, "sscol": sscol, "ident": ident}
               for b in range(NCORES)]
    res = run_bass_kernel_spmd(progs["c"], in_maps, core_ids=cores)
    x3 = [np.asarray(r["x3"]) for r in res.results]

    gate_t = _tile_gateup(np.asarray(L1_ffn_gate_w, dtype=np.float32))
    up_t = _tile_gateup(np.asarray(L1_ffn_up_w, dtype=np.float32))
    down_t = np.ascontiguousarray(np.asarray(L1_ffn_down_w, dtype=np.float32).T)
    gs = np.ascontiguousarray(
        np.asarray(L1_ffn_gate_s, dtype=np.float32).reshape(44, 128).T)
    us = np.ascontiguousarray(
        np.asarray(L1_ffn_up_s, dtype=np.float32).reshape(44, 128).T)
    ds = np.ascontiguousarray(np.asarray(L1_ffn_down_s, dtype=np.float32))
    in_maps = [{"x3": x3[b], "gate_t": gate_t, "up_t": up_t, "down_t": down_t,
                "gs": gs, "us": us, "ds": ds, "ident": ident}
               for b in range(NCORES)]
    res = run_bass_kernel_spmd(progs["d"], in_maps, core_ids=cores)
    x4 = [np.asarray(r["x4"]) for r in res.results]
    return np.stack(x4).astype(np.float32)



def build_phase_b(nc, ctx, P, S, x1_in, factors_in, bias_in, x2_out, ident_in,
                  fsum_pre=None):
    t, v, sc = S['t'], S['v'], S['s']
    qmi, gms = S['qmi'], S['gms']
    qx = [S['qx0'], S['qx1']]
    qst = [S['qst0'], S['qst1']]
    gfc = [S['gfc0'], S['gfc1'], S['gfc2'], S['gfc3']]

    id_sb = ctx.enter_context(nc.sbuf_tensor("id_sbB", [128, 128], BF16))
    xbuf = ctx.enter_context(nc.sbuf_tensor("xbufB", [128, 2 * D], F32))
    trash = ctx.enter_context(nc.sbuf_tensor("trashB", [128, D], F32))
    s1 = ctx.enter_context(nc.sbuf_tensor("s1B", [128, NT], F32))
    s2 = ctx.enter_context(nc.sbuf_tensor("s2B", [128, NT], F32))
    rstd = ctx.enter_context(nc.sbuf_tensor("rstdB", [128, NT], F32))
    nbias = ctx.enter_context(nc.sbuf_tensor("nbiasB", [128, NT], F32))
    tmp = ctx.enter_context(nc.sbuf_tensor("tmpB", [128, 3 * NT], F32))
    facc = ctx.enter_context(nc.sbuf_tensor("facc", [128, 3 * D], F32))
    f_sb = ctx.enter_context(nc.sbuf_tensor("f_sb", [128, ND * D], BF16))
    m_bf = ctx.enter_context(nc.sbuf_tensor("m_bf", [128, NT * D], BF16))
    zrow = ctx.enter_context(nc.sbuf_tensor("zrow", [128, 2 * D], BF16))
    zt = ctx.enter_context(nc.sbuf_tensor("ztB", [128, 2 * D], BF16))
    ones8 = ctx.enter_context(nc.sbuf_tensor("ones8", [1, 128], F32))
    brow = ctx.enter_context(nc.sbuf_tensor("brow", [1, D], F32))
    bias8 = ctx.enter_context(nc.sbuf_tensor("bias8", [128, D], BF16))

    ps_m = ctx.enter_context(nc.psum_tensor("ps_m", [128, 2048], F32))
    ps_t0 = ctx.enter_context(nc.psum_tensor("ps_t0B", [128, 1024], BF16))
    ps_t1 = ctx.enter_context(nc.psum_tensor("ps_t1B", [128, 1024], BF16))
    ps_tt = [ps_t0, ps_t1]
    ps_b = ctx.enter_context(nc.psum_tensor("ps_bB", [128, 512], F32))

    if fsum_pre is not None:
        fsum_ap, fsum_wait = fsum_pre
        gj = gfc[1]

        def fld(e, fw=fsum_wait, gh=gj.h):
            e.wait_ge(fw[0], fw[1])
            e.dma_start(out=f_sb[:, :].rearrange("p (dt c) -> p dt c", c=D),
                        in_=fsum_ap.rearrange("(dt p) c -> p dt c", p=128)
                        ).then_inc(gh, 16)
        P.g(fld)
        f_ready = [(gj.h, gj.inc(16))]
        emit_fchain = False
    else:
        emit_fchain = True
    fcast_g = []
    for dt in range(ND if emit_fchain else 0):
        j = dt % 3
        gj = gfc[j]
        base = gj.n + 16

        def fchain(e, dt=dt, j=j, fg=list(fcast_g), base=base, gh=gj.h):
            if dt >= 3:
                e.wait_ge(gh, fg[dt - 3])
            e.dma_start(out=facc[:, j * D:(j + 1) * D],
                        in_=factors_in[0, dt * 128:(dt + 1) * 128, :]
                        ).then_inc(gh, 16)
            for r in range(1, 8):
                e.wait_ge(gh, base + (r - 1) * 16)
                e.dma_start(out=facc[:, j * D:(j + 1) * D],
                            in_=factors_in[r, dt * 128:(dt + 1) * 128, :],
                            accum_op=ALU.add).then_inc(gh, 16)
            e.wait_ge(gh, base + 7 * 16)
            e.dma_start(out=f_sb[:, dt * D:(dt + 1) * D],
                        in_=facc[:, j * D:(j + 1) * D]).then_inc(gh, 16)
        P.g(fchain)
        gj.inc(16 * 9)
        fcast_g.append(gj.n)
    if emit_fchain:
        f_ready = [(s_.h, s_.n) for s_ in gfc]

    def ld_idB(e):
        e.dma_start(out=id_sb[:, :], in_=ident_in).then_inc(qmi.h, 16)
        e.dma_start(out=brow[0:1, :], in_=bias_in[None, :]).then_inc(qmi.h, 16)
    P.y(ld_idB)
    idb_ready = qmi.inc(32)

    def mset(e):
        e.memset(ones8[:, :], float(np.sqrt(8.0)))
    P.g(mset)

    def mset_bar(e):
        e.dma_start(out=ones8[0:1, 0:8], in_=ones8[0:1, 0:8]).then_inc(gms.h, 16)
    P.g(mset_bar)
    ones_ready = gms.inc(16)

    bev_v = []
    for et in range(4):
        def bbc(e, et=et, idb=idb_ready, onr=ones_ready, bv=list(bev_v)):
            if et == 0:
                e.wait_ge(qmi.h, idb)
                e.wait_ge(gms.h, onr)
            else:
                e.wait_ge(v.h, bv[-1])
            e.matmul(ps_b[:, :], ones8[0:1, :],
                     brow[0:1, et * 512:(et + 1) * 512],
                     start=True, stop=True).then_inc(t.h, 1)
        P.t(bbc)
        btv = t.inc()

        def bev(e, et=et, btv=btv):
            e.wait_ge(t.h, btv)
            e.tensor_copy(bias8[:, et * 512:(et + 1) * 512],
                          ps_b[:, :]).then_inc(v.h, 1)
        P.v(bev)
        bev_v.append(v.inc())
    bias8_v = bev_v[-1]

    statB_v, statB_s = [], []
    for k in range(NT):
        j = k % 2

        def ld(e, k=k, j=j, sv=list(statB_v), ss=list(statB_s), qh=qx[j].h):
            if k >= 2:
                e.wait_ge(v.h, sv[k - 2])
                e.wait_ge(sc.h, ss[k - 2])
            e.dma_start(out=xbuf[:, j * D:(j + 1) * D],
                        in_=x1_in[k * 128:(k + 1) * 128, :]).then_inc(qh, 16)
        P.y(ld)
        dw = (qx[j].h, qx[j].inc(16))
        vv, sv_ = emit_stats_tile(P, S, xbuf[:, j * D:(j + 1) * D], k,
                                  s1, s2, trash[:, :], dw)
        statB_v.append(vv)
        statB_s.append(sv_)

    lnB = emit_ln_finalize(P, S, s1, s2, rstd, nbias, tmp, D, EPS, NT,
                           wait=[(v.h, statB_v[-1]), (sc.h, statB_s[-1])])

    mstat_v, mstat_s = [], []
    mev_v = []
    norm_sB = []
    tp_vB = []
    for k in range(NT):
        j = k % 2

        def ldx(e, k=k, j=j, ns=list(norm_sB), qh=qx[j].h,
                pv=statB_v[-1], ps=statB_s[-1]):
            if k >= 2:
                e.wait_ge(sc.h, ns[k - 2])
            else:
                e.wait_ge(v.h, pv)
                e.wait_ge(sc.h, ps)
            e.dma_start(out=xbuf[:, j * D:(j + 1) * D],
                        in_=x1_in[k * 128:(k + 1) * 128, :]).then_inc(qh, 16)
        P.y(ldx)
        dxw = (qx[j].h, qx[j].inc(16))

        def nrm(e, k=k, j=j, dxw=dxw, lnB=lnB, tv=list(tp_vB)):
            e.wait_ge(dxw[0], dxw[1])
            e.wait_ge(v.h, lnB)
            if k >= 2:
                e.wait_ge(v.h, tv[-3])
            e.activation(zrow[:, j * D:(j + 1) * D],
                         xbuf[:, j * D:(j + 1) * D],
                         AF.Identity, bias=nbias[:, k:k + 1],
                         scale=rstd[:, k:k + 1]).then_inc(sc.h, 1)
        P.s(nrm)
        norm_sB.append(sc.inc())

        for grp in range(2):
            pst = ps_tt[grp]

            def tp(e, k=k, j=j, grp=grp, pst=pst, nsv=norm_sB[-1],
                   idb=idb_ready, tv=list(tp_vB)):
                e.wait_ge(sc.h, nsv)
                if k == 0 and grp == 0:
                    e.wait_ge(qmi.h, idb)
                if len(tv) >= 2:
                    e.wait_ge(v.h, tv[-2])
                for jj in range(8):
                    dt = grp * 8 + jj
                    ins = e.transpose(
                        pst[:, jj * 128:(jj + 1) * 128],
                        zrow[:, j * D + dt * 128:j * D + (dt + 1) * 128],
                        id_sb[:, :])
                    if jj == 7:
                        ins.then_inc(t.h, 1)
            P.t(tp)
            tval = t.inc()

            def ev(e, j=j, grp=grp, pst=pst, tval=tval):
                e.wait_ge(t.h, tval)
                for jj in range(8):
                    dt = grp * 8 + jj
                    ins = e.tensor_copy(
                        zt[:, j * D + dt * 128:j * D + (dt + 1) * 128],
                        pst[:, jj * 128:(jj + 1) * 128])
                    if jj == 7:
                        ins.then_inc(v.h, 1)
            P.v(ev)
            tp_vB.append(v.inc())

        def mm(e, k=k, j=j, ztv=tp_vB[-1], fr=tuple(f_ready), mv=list(mev_v)):
            e.wait_ge(v.h, ztv)
            for fh, fv in fr:
                e.wait_ge(fh, fv)
            if mv:
                e.wait_ge(v.h, mv[-1])
            for et in range(4):
                for dt in range(ND):
                    ins = e.matmul(
                        ps_m[:, et * 512:(et + 1) * 512],
                        zt[:, j * D + dt * 128:j * D + (dt + 1) * 128],
                        f_sb[:, dt * D + et * 512:dt * D + (et + 1) * 512],
                        start=(dt == 0), stop=(dt == ND - 1))
                    if et == 3 and dt == ND - 1:
                        ins.then_inc(t.h, 1)
        P.t(mm)
        mmv = t.inc()

        def mev(e, k=k, mmv=mmv, b8=bias8_v):
            e.wait_ge(t.h, mmv)
            e.wait_ge(v.h, b8)
            e.tensor_tensor(m_bf[:, k * D:(k + 1) * D], ps_m[:, :],
                            bias8[:, :], op=ALU.add).then_inc(v.h, 1)
        P.v(mev)
        mev_v.append(v.inc())

        def mst_v(e, k=k, mv=mev_v[-1]):
            e.wait_ge(v.h, mv)
            e.reduce_sum(s1[:, k:k + 1], m_bf[:, k * D:(k + 1) * D],
                         axis=AX.X).then_inc(v.h, 1)
        P.v(mst_v)
        mstat_v.append(v.inc())

        def mst_s(e, k=k, mv=mev_v[-1]):
            e.wait_ge(v.h, mv)
            e.activation(trash[:, :], m_bf[:, k * D:(k + 1) * D], AF.Square,
                         accum_out=s2[:, k:k + 1]).then_inc(sc.h, 1)
        P.s(mst_s)
        mstat_s.append(sc.inc())

    lnM = emit_ln_finalize(P, S, s1, s2, rstd, nbias, tmp, D, 8 * EPS, NT,
                           wait=[(v.h, mstat_v[-1]), (sc.h, mstat_s[-1])])

    st2_q = []
    addB_v = []
    nsl = norm_sB[-1]
    for k in range(NT):
        j = k % 2

        def ldx(e, k=k, j=j, sq=list(st2_q), nsl=nsl, qh=qx[j].h,
                sh=qst[j].h):
            if k >= 2:
                e.wait_ge(sh, sq[k - 2])
            else:
                e.wait_ge(sc.h, nsl)
            e.dma_start(out=xbuf[:, j * D:(j + 1) * D],
                        in_=x1_in[k * 128:(k + 1) * 128, :]).then_inc(qh, 16)
        P.y(ldx)
        dxw = (qx[j].h, qx[j].inc(16))

        def nrmM(e, k=k, lnM=lnM, av=list(addB_v)):
            e.wait_ge(v.h, lnM)
            if av:
                e.wait_ge(v.h, av[-1])
            e.activation(trash[:, :], m_bf[:, k * D:(k + 1) * D], AF.Identity,
                         bias=nbias[:, k:k + 1],
                         scale=rstd[:, k:k + 1]).then_inc(sc.h, 1)
        P.s(nrmM)
        nv = sc.inc()

        def addB(e, k=k, j=j, nv=nv, dxw=dxw):
            e.wait_ge(sc.h, nv)
            e.wait_ge(dxw[0], dxw[1])
            e.tensor_tensor(xbuf[:, j * D:(j + 1) * D],
                            xbuf[:, j * D:(j + 1) * D], trash[:, :],
                            op=ALU.add).then_inc(v.h, 1)
        P.v(addB)
        addB_v.append(v.inc())

        def stx(e, k=k, j=j, av=addB_v[-1], sh=qst[j].h):
            e.wait_ge(v.h, av)
            e.dma_start(out=x2_out[k * 128:(k + 1) * 128, :],
                        in_=xbuf[:, j * D:(j + 1) * D]).then_inc(sh, 16)
        P.y(stx)
        st2_q.append(qst[j].inc(16))

    def finB(e, l0=qst[0].n, l1=qst[1].n):
        e.wait_ge(qst[0].h, l0)
        e.wait_ge(qst[1].h, l1)
    P.y(finB)


def build_b_program():
    nc = bass.Bass("TRN2", target_bir_lowering=False)
    x1_in = nc.dram_tensor("x1", [T, D], F32, kind="ExternalInput").ap()
    facs = nc.dram_tensor("factors", [8, D, D], F32, kind="ExternalInput").ap()
    bias_in = nc.dram_tensor("fbias", [D], F32, kind="ExternalInput").ap()
    ident_in = nc.dram_tensor("ident", [128, 128], BF16,
                              kind="ExternalInput").ap()
    x2_out = nc.dram_tensor("x2", [T, D], F32, kind="ExternalOutput").ap()
    with contextlib.ExitStack() as ctx:
        S = mk_sems(nc, ctx)
        P = Prog(nc)
        build_phase_b(nc, ctx, P, S, x1_in, facs, bias_in, x2_out, ident_in)
        P.emit()
    return nc



H = 32
DH = 64
G = 2


def build_phase_c(nc, ctx, P, S, x2_in, wqt_in, wkt_in, wvt_in, wot_in,
                  ns_in, ss_in, x3_out, ident_in):
    t, v, sc = S['t'], S['v'], S['s']
    qmi, gms, gkv, gwq, gwo = (S['qmi'], S['gms'], S['gkv'], S['gwq'],
                               S['gwo'])
    qx = [S['qx0'], S['qx1']]
    qst = [S['qst0'], S['qst1']]

    id_sb = ctx.enter_context(nc.sbuf_tensor("id_sbC", [128, 128], BF16))
    xbuf = ctx.enter_context(nc.sbuf_tensor("xbufC", [128, 2 * D], F32))
    trash = ctx.enter_context(nc.sbuf_tensor("trashC", [128, D], F32))
    zres = ctx.enter_context(nc.sbuf_tensor("zresC", [128, D], F32))
    s1 = ctx.enter_context(nc.sbuf_tensor("s1C", [128, NT], F32))
    s2 = ctx.enter_context(nc.sbuf_tensor("s2C", [128, NT], F32))
    rstd = ctx.enter_context(nc.sbuf_tensor("rstdC", [128, NT], F32))
    nbias = ctx.enter_context(nc.sbuf_tensor("nbiasC", [128, NT], F32))
    tmp = ctx.enter_context(nc.sbuf_tensor("tmpC", [128, 5 * NT], F32))
    rncol = ctx.enter_context(nc.sbuf_tensor("rncol", [128, NT], F32))
    rnr = ctx.enter_context(nc.sbuf_tensor("rnr", [128, NT], F32))
    rnb = ctx.enter_context(nc.sbuf_tensor("rnb", [128, NT], F32))
    nsss = ctx.enter_context(nc.sbuf_tensor("nsss", [128, ND], F32))
    poolA = ctx.enter_context(nc.sbuf_tensor("poolA", [128, ND * T], BF16))
    poolB = ctx.enter_context(nc.sbuf_tensor("poolB", [128, ND * T], BF16))
    esb = ctx.enter_context(nc.sbuf_tensor("esbC", [128, NT * 512], BF16))
    kt = ctx.enter_context(nc.sbuf_tensor("ktC", [128, T], BF16))
    vsb = ctx.enter_context(nc.sbuf_tensor("vsbC", [128, NT * 130], BF16))
    rs_sb = ctx.enter_context(nc.sbuf_tensor("rs_sb", [128, 1024], F32))
    ones64 = ctx.enter_context(nc.sbuf_tensor("ones64", [128, 64], F32))
    wctx = contextlib.ExitStack()
    kvw = wctx.enter_context(nc.sbuf_tensor("kvw", [128, ND * 256], BF16))
    wqs = wctx.enter_context(nc.sbuf_tensor("wqs", [128, ND * 256], BF16))

    def ld0(e):
        e.dma_start(out=id_sb[:, :], in_=ident_in).then_inc(qmi.h, 16)
        e.dma_start(out=nsss[:, :], in_=ns_in).then_inc(qmi.h, 16)
        e.dma_start(out=tmp[:, 4 * NT:5 * NT], in_=ss_in).then_inc(qmi.h, 16)
    P.y(ld0)
    c0_q = qmi.inc(48)

    def mset1(e):
        e.memset(ones64[:, :], 1.0)
        e.memset(vsb[:, :], 1.0)
        e.dma_start(out=ones64[0:1, 0:8], in_=ones64[0:1, 0:8]).then_inc(gms.h, 16)
    P.g(mset1)
    mset_g = gms.inc(16)

    def nsmul(e, c0=c0_q):
        e.wait_ge(qmi.h, c0)
        e.tensor_mul(nsss[:, :], nsss[:, :],
                     tmp[:, 4 * NT:5 * NT]).then_inc(v.h, 1)
    P.v(nsmul)
    nsss_v = v.inc()

    def ldkv(e, nv=nsss_v):
        src_k = wkt_in.rearrange("(dt p) c -> p dt c", p=128)
        src_v = wvt_in.rearrange("(dt p) c -> p dt c", p=128)
        dst = kvw[:, :].rearrange("p (dt c) -> p dt c", c=256)
        e.dma_start(out=dst[:, :, 0:128], in_=src_k).then_inc(gkv.h, 16)
        e.dma_start(out=dst[:, :, 128:256], in_=src_v).then_inc(gkv.h, 16)
    P.g(ldkv)
    kv_g = gkv.inc(32)

    def sckv(e, kv_g=kv_g, nv=nsss_v):
        e.wait_ge(gkv.h, kv_g)
        e.wait_ge(v.h, nv)
        for dt in range(ND):
            ins = e.tensor_scalar_mul(kvw[:, dt * 256:(dt + 1) * 256],
                                      kvw[:, dt * 256:(dt + 1) * 256],
                                      nsss[:, dt:dt + 1])
            if dt == ND - 1:
                ins.then_inc(v.h, 1)
    P.v(sckv)
    kvw_v = v.inc()

    statC_v, statC_s = [], []
    for k in range(NT):
        j = k % 2

        def ld(e, k=k, j=j, sv=list(statC_v), ss=list(statC_s), qh=qx[j].h):
            if k >= 2:
                e.wait_ge(v.h, sv[k - 2])
                e.wait_ge(sc.h, ss[k - 2])
            e.dma_start(out=xbuf[:, j * D:(j + 1) * D],
                        in_=x2_in[k * 128:(k + 1) * 128, :]).then_inc(qh, 16)
        P.y(ld)
        dw = (qx[j].h, qx[j].inc(16))
        vv, sv_ = emit_stats_tile(P, S, xbuf[:, j * D:(j + 1) * D], k,
                                  s1, s2, trash[:, :], dw)
        statC_v.append(vv)
        statC_s.append(sv_)

    lnC = emit_ln_finalize(P, S, s1, s2, rstd, nbias, tmp, D, EPS, NT,
                           wait=[(v.h, statC_v[-1]), (sc.h, statC_s[-1])])

    varr = tmp[:, 3 * NT:4 * NT]

    def rn1(e, lnC=lnC):
        e.wait_ge(v.h, lnC)
        e.tensor_scalar_add(varr, tmp[:, NT:2 * NT],
                            -float(EPS)).then_inc(v.h, 1)
    P.v(rn1)
    rn1v = v.inc()

    def rn2(e, rn1v=rn1v):
        e.wait_ge(v.h, rn1v)
        e.activation(varr, varr, AF.Sqrt, bias=0.0, scale=1.0).then_inc(sc.h, 1)
    P.s(rn2)
    rn2s = sc.inc()

    def rn3(e, rn2s=rn2s):
        e.wait_ge(sc.h, rn2s)
        e.tensor_mul(varr, varr, rstd[:, 0:NT]).then_inc(v.h, 1)
    P.v(rn3)
    rn3v = v.inc()

    def rn4(e, rn3v=rn3v):
        e.wait_ge(v.h, rn3v)
        e.tensor_scalar_add(varr, varr, 1e-8).then_inc(v.h, 1)
    P.v(rn4)
    rn4v = v.inc()

    def rn5(e, rn4v=rn4v):
        e.wait_ge(v.h, rn4v)
        e.reciprocal(rncol[:, :], varr).then_inc(v.h, 1)
    P.v(rn5)
    rn5v = v.inc()

    def rn6(e, rn5v=rn5v):
        e.wait_ge(v.h, rn5v)
        e.tensor_mul(rnr[:, :], rncol[:, :], rstd[:, 0:NT])
        e.tensor_mul(rnb[:, :], rncol[:, :], nbias[:, 0:NT]).then_inc(v.h, 1)
    P.v(rn6)
    rn_ready = v.inc()

    with contextlib.ExitStack() as pctx:
        ps_t0 = pctx.enter_context(nc.psum_tensor("ps_t0C", [128, 1024], BF16))
        ps_t1 = pctx.enter_context(nc.psum_tensor("ps_t1C", [128, 1024], BF16))
        ps_tt = [ps_t0, ps_t1]
        ps_q0 = pctx.enter_context(nc.psum_tensor("ps_q0C", [128, 512], F32))
        ps_q1 = pctx.enter_context(nc.psum_tensor("ps_q1C", [128, 512], F32))
        ps_qq = [ps_q0, ps_q1]

        zh_s = []
        tpC_v = []
        for k in range(NT):
            j = k % 2

            def ldx(e, k=k, j=j, zs=list(zh_s), qh=qx[j].h,
                    pv=statC_v[-1], ps=statC_s[-1]):
                if k >= 2:
                    e.wait_ge(sc.h, zs[k - 2])
                else:
                    e.wait_ge(v.h, pv)
                    e.wait_ge(sc.h, ps)
                e.dma_start(out=xbuf[:, j * D:(j + 1) * D],
                            in_=x2_in[k * 128:(k + 1) * 128, :]
                            ).then_inc(qh, 16)
            P.y(ldx)
            dxw = (qx[j].h, qx[j].inc(16))

            def zh(e, k=k, j=j, dxw=dxw, rr=rn_ready, tv=list(tpC_v)):
                e.wait_ge(dxw[0], dxw[1])
                e.wait_ge(v.h, rr)
                if k >= 2:
                    e.wait_ge(v.h, tv[-3])
                e.activation(esb[:, j * D:(j + 1) * D],
                             xbuf[:, j * D:(j + 1) * D],
                             AF.Identity, bias=rnb[:, k:k + 1],
                             scale=rnr[:, k:k + 1]).then_inc(sc.h, 1)
            P.s(zh)
            zh_s.append(sc.inc())

            for grp in range(2):
                pst = ps_tt[grp]

                def tp(e, k=k, j=j, grp=grp, pst=pst, zs=zh_s[-1],
                       c0=c0_q, tv=list(tpC_v)):
                    e.wait_ge(sc.h, zs)
                    if k == 0 and grp == 0:
                        e.wait_ge(qmi.h, c0)
                    if len(tv) >= 2:
                        e.wait_ge(v.h, tv[-2])
                    for jj in range(8):
                        dt = grp * 8 + jj
                        ins = e.transpose(
                            pst[:, jj * 128:(jj + 1) * 128],
                            esb[:, j * D + dt * 128:j * D + (dt + 1) * 128],
                            id_sb[:, :])
                        if jj == 7:
                            ins.then_inc(t.h, 1)
                P.t(tp)
                tval = t.inc()

                def ev(e, k=k, grp=grp, pst=pst, tval=tval):
                    e.wait_ge(t.h, tval)
                    for jj in range(8):
                        dt = grp * 8 + jj
                        ins = e.tensor_copy(
                            poolA[:, dt * T + k * 128:dt * T + (k + 1) * 128],
                            pst[:, jj * 128:(jj + 1) * 128])
                        if jj == 7:
                            ins.then_inc(v.h, 1)
                P.v(ev)
                tpC_v.append(v.inc())
        ht_v = tpC_v[-1]

        kev_v = []
        for s_ in range(4):
            par = s_ % 2

            def kmm(e, s_=s_, par=par, hv=ht_v, kv=kvw_v, kev=list(kev_v)):
                e.wait_ge(v.h, hv)
                e.wait_ge(v.h, kv)
                if len(kev) >= 2:
                    e.wait_ge(v.h, kev[-2])
                for dt in range(ND):
                    ins = e.matmul(
                        ps_qq[par][:, :],
                        kvw[:, dt * 256:dt * 256 + 128],
                        poolA[:, dt * T + s_ * 512:dt * T + (s_ + 1) * 512],
                        start=(dt == 0), stop=(dt == ND - 1))
                    if dt == ND - 1:
                        ins.then_inc(t.h, 1)
            P.t(kmm)
            ktv = t.inc()

            def kev_(e, s_=s_, par=par, ktv=ktv):
                e.wait_ge(t.h, ktv)
                e.tensor_copy(kt[:, s_ * 512:(s_ + 1) * 512],
                              ps_qq[par][:, :]).then_inc(v.h, 1)
            P.v(kev_)
            kev_v.append(v.inc())

        vev_v = []
        for k in range(NT):
            par = k % 2

            def vmm(e, k=k, par=par, hv=ht_v, kv=kvw_v, kl=kev_v[-1],
                    vv=list(vev_v)):
                if k == 0:
                    e.wait_ge(v.h, kl)
                e.wait_ge(v.h, hv)
                e.wait_ge(v.h, kv)
                if len(vv) >= 2:
                    e.wait_ge(v.h, vv[-2])
                for dt in range(ND):
                    ins = e.matmul(
                        ps_qq[par][:, 0:128],
                        poolA[:, dt * T + k * 128:dt * T + (k + 1) * 128],
                        kvw[:, dt * 256 + 128:(dt + 1) * 256],
                        start=(dt == 0), stop=(dt == ND - 1))
                    if dt == ND - 1:
                        ins.then_inc(t.h, 1)
            P.t(vmm)
            vtv = t.inc()

            def vev(e, k=k, par=par, vtv=vtv, mg=mset_g):
                e.wait_ge(t.h, vtv)
                e.wait_ge(gms.h, mg)
                e.tensor_copy(vsb[:, k * 130:k * 130 + 64],
                              ps_qq[par][:, 0:64])
                e.tensor_copy(vsb[:, k * 130 + 65:k * 130 + 129],
                              ps_qq[par][:, 64:128]).then_inc(v.h, 1)
            P.v(vev)
            vev_v.append(v.inc())
        v_ready = vev_v[-1]

        qev_v = []
        for sg in range(8):
            def ldwq(e, sg=sg, nv=nsss_v, qe=list(qev_v)):
                if qe:
                    e.wait_ge(v.h, qe[-1])
                src = wqt_in[:, sg * 256:(sg + 1) * 256]
                src = src.rearrange("(dt p) c -> p dt c", p=128)
                e.dma_start(out=wqs[:, :].rearrange("p (dt c) -> p dt c", c=256),
                            in_=src).then_inc(gwq.h, 16)
            P.g(ldwq)
            wq_g = gwq.inc(16)

            def scwq(e, wq_g=wq_g, nv=nsss_v):
                e.wait_ge(gwq.h, wq_g)
                e.wait_ge(v.h, nv)
                for dt in range(ND):
                    ins = e.tensor_scalar_mul(wqs[:, dt * 256:(dt + 1) * 256],
                                              wqs[:, dt * 256:(dt + 1) * 256],
                                              nsss[:, dt:dt + 1])
                    if dt == ND - 1:
                        ins.then_inc(v.h, 1)
            P.v(scwq)
            wqs_v = v.inc()

            for mt_l in range(2):
                mt = sg * 2 + mt_l
                for s_ in range(4):
                    par = (mt_l * 4 + s_) % 2

                    def qmm(e, mt=mt, mt_l=mt_l, s_=s_, par=par, wv=wqs_v,
                            hv=ht_v, vl=v_ready, qe=list(qev_v)):
                        e.wait_ge(v.h, wv)
                        e.wait_ge(v.h, hv)
                        if len(qe) < 2:
                            e.wait_ge(v.h, vl)
                        else:
                            e.wait_ge(v.h, qe[-2])
                        for dt in range(ND):
                            ins = e.matmul(
                                ps_qq[par][:, :],
                                wqs[:, dt * 256 + mt_l * 128:
                                    dt * 256 + (mt_l + 1) * 128],
                                poolA[:, dt * T + s_ * 512:dt * T + (s_ + 1) * 512],
                                start=(dt == 0), stop=(dt == ND - 1))
                            if dt == ND - 1:
                                ins.then_inc(t.h, 1)
                    P.t(qmm)
                    qtv = t.inc()

                    def qev(e, mt=mt, s_=s_, par=par, qtv=qtv):
                        e.wait_ge(t.h, qtv)
                        for half in range(2):
                            h_ = 2 * mt + half
                            u = h_ % 16
                            gb = 64 * (h_ // 16)
                            ins = e.tensor_copy(
                                poolB[gb:gb + 64,
                                      u * T + s_ * 512:u * T + (s_ + 1) * 512],
                                ps_qq[par][half * 64:half * 64 + 64, :])
                            if half == 1:
                                ins.then_inc(v.h, 1)
                    P.v(qev)
                    qev_v.append(v.inc())
        q_ready = qev_v[-1]
    wctx.close()
    esb2 = ctx.enter_context(nc.sbuf_tensor("esb2C", [128, NT * 512], BF16))
    esbs = [esb, esb2]

    with contextlib.ExitStack() as pctx:
        ps_sc0 = pctx.enter_context(nc.psum_tensor("ps_sc0", [128, 1024], F32))
        ps_sc1 = pctx.enter_context(nc.psum_tensor("ps_sc1", [128, 1024], F32))
        ps_scc = [ps_sc0, ps_sc1]
        ps_av0 = pctx.enter_context(nc.psum_tensor("ps_av0", [128, 512], F32))
        ps_av1 = pctx.enter_context(nc.psum_tensor("ps_av1", [128, 512], F32))
        ps_avv = [ps_av0, ps_av1]
        ps_rep = pctx.enter_context(nc.psum_tensor("ps_rep", [128, 512], F32))

        exp_s = []
        last_exp = []
        av_t = []
        rec_vs = []
        rep_t = []
        norm_v = []
        wot_g = []

        hc_list = [(h_, c_) for h_ in range(H) for c_ in range(4)]
        NIDX = len(hc_list)

        def emit_scores(idx):
            h_, c_ = hc_list[idx]
            g_ = h_ // 16
            kb = g_ * 64
            for tg in range(8):
                par = tg % 2

                def smm(e, h_=h_, c_=c_, tg=tg, par=par, kb=kb,
                        qr=q_ready, es=list(exp_s), idx=idx,
                        av=list(av_t)):
                    if idx == 0 and tg == 0:
                        e.wait_ge(v.h, qr)
                    if len(es) >= 2:
                        e.wait_ge(sc.h, es[-2])
                    u = h_ % 16
                    for half in range(2):
                        tt = tg * 2 + half
                        ins = e.matmul(
                            ps_scc[par][:, half * 512:(half + 1) * 512],
                            kt[kb:kb + 64, tt * 128:(tt + 1) * 128],
                            poolB[kb:kb + 64,
                                  u * T + c_ * 512:u * T + (c_ + 1) * 512],
                            start=True, stop=True)
                        if half == 1:
                            ins.then_inc(t.h, 1)
                P.t(smm)
                stv = t.inc()

                def exf(e, tg=tg, par=par, stv=stv, idx=idx,
                        eb=esbs[idx % 2], av=list(av_t)):
                    e.wait_ge(t.h, stv)
                    if tg == 0 and idx >= 2:
                        e.wait_ge(t.h, av[idx - 2])
                    e.activation(eb[:, tg * 1024:(tg + 1) * 1024],
                                 ps_scc[par][:, :], AF.Exp,
                                 bias=0.0, scale=0.125).then_inc(sc.h, 1)
                P.s(exf)
                exp_s.append(sc.inc())
            last_exp.append(exp_s[-1])
            if h_ >= 16 and c_ == 3:
                def ldwo(e, u=h_ - 16, stv=stv):
                    e.wait_ge(t.h, stv)
                    e.dma_start(out=poolB[:, u * T:(u + 1) * T],
                                in_=wot_in[u * 128:(u + 1) * 128, :]
                                ).then_inc(gwo.h, 16)
                P.g(ldwo)
                wot_g.append(gwo.inc(16))

        def emit_av(idx):
            h_, c_ = hc_list[idx]
            g_ = h_ // 16
            mt = h_ // 2
            qb = (h_ % 2) * 64
            par_av = idx % 2
            vcol = 65 * g_

            def avm(e, idx=idx, par_av=par_av, vcol=vcol, pe=last_exp[idx],
                    nv=list(norm_v), vl=v_ready, eb=esbs[idx % 2]):
                e.wait_ge(sc.h, pe)
                e.wait_ge(v.h, vl)
                if len(nv) >= 2:
                    e.wait_ge(v.h, nv[-2])
                for tt in range(NT):
                    ins = e.matmul(
                        ps_avv[par_av][0:65, :],
                        vsb[:, tt * 130 + vcol:tt * 130 + vcol + 65],
                        eb[:, tt * 512:(tt + 1) * 512],
                        start=(tt == 0), stop=(tt == NT - 1))
                    if tt == NT - 1:
                        ins.then_inc(t.h, 1)
            P.t(avm)
            av_t.append(t.inc())

            def rec(e, par_av=par_av, atv=av_t[-1]):
                e.wait_ge(t.h, atv)
                e.reciprocal(rs_sb[64:65, par_av * 512:(par_av + 1) * 512],
                             ps_avv[par_av][64:65, :]).then_inc(v.h, 1)
            P.v(rec)
            rec_vs.append(v.inc())

            def ocp(e, c_=c_, par_av=par_av, mt=mt, qb=qb, atv=av_t[-1]):
                e.wait_ge(t.h, atv)
                e.tensor_copy(
                    poolA[qb:qb + 64, mt * T + c_ * 512:mt * T + (c_ + 1) * 512],
                    ps_avv[par_av][0:64, :]).then_inc(v.h, 1)
            P.v(ocp)
            return v.inc()

        ocp_vs = {}

        def emit_rep(idx):
            h_, c_ = hc_list[idx]
            mt = h_ // 2
            qb = (h_ % 2) * 64

            def rep(e, rv=rec_vs[idx], nv=list(norm_v), par_av=idx % 2):
                e.wait_ge(v.h, rv)
                if len(nv) >= 1:
                    e.wait_ge(v.h, nv[-1])
                e.matmul(ps_rep[0:64, :], ones64[64:65, :],
                         rs_sb[64:65, par_av * 512:(par_av + 1) * 512],
                         start=True, stop=True).then_inc(t.h, 1)
            P.t(rep)
            rep_t.append(t.inc())

            def osc(e, c_=c_, mt=mt, qb=qb, ocv=ocp_vs[idx], rtv=rep_t[-1]):
                e.wait_ge(v.h, ocv)
                e.wait_ge(t.h, rtv)
                e.tensor_tensor(
                    poolA[qb:qb + 64, mt * T + c_ * 512:mt * T + (c_ + 1) * 512],
                    poolA[qb:qb + 64, mt * T + c_ * 512:mt * T + (c_ + 1) * 512],
                    ps_rep[0:64, :], op=ALU.mult).then_inc(v.h, 1)
            P.v(osc)
            norm_v.append(v.inc())

        for idx in range(NIDX + 2):
            if idx < NIDX:
                emit_scores(idx)
            if 1 <= idx <= NIDX:
                ocp_vs[idx - 1] = emit_av(idx - 1)
            if idx >= 2:
                emit_rep(idx - 2)
        o_ready = norm_v[-1]
        wot_ready = wot_g[-1]

    with contextlib.ExitStack() as pctx:
        ps_wo = pctx.enter_context(nc.psum_tensor("ps_wo", [128, 2048], F32))

        ep_v = []
        st3_q = []
        for k in range(NT):
            j = k % 2

            def ldx(e, k=k, j=j, sq=list(st3_q), zs=zh_s[-1], qh=qx[j].h,
                    sh=qst[j].h):
                if k >= 2:
                    e.wait_ge(sh, sq[k - 2])
                else:
                    e.wait_ge(sc.h, zs)
                e.dma_start(out=xbuf[:, j * D:(j + 1) * D],
                            in_=x2_in[k * 128:(k + 1) * 128, :]
                            ).then_inc(qh, 16)
            P.y(ldx)
            dxw = (qx[j].h, qx[j].inc(16))

            def womm(e, k=k, orv=o_ready, wr=wot_ready, ep=list(ep_v)):
                e.wait_ge(v.h, orv)
                e.wait_ge(gwo.h, wr)
                if ep:
                    e.wait_ge(v.h, ep[-1])
                for oc in range(4):
                    for dqt in range(ND):
                        ins = e.matmul(
                            ps_wo[:, oc * 512:(oc + 1) * 512],
                            poolA[:, dqt * T + k * 128:dqt * T + (k + 1) * 128],
                            poolB[:, dqt * T + oc * 512:dqt * T + (oc + 1) * 512],
                            start=(dqt == 0), stop=(dqt == ND - 1))
                        if oc == 3 and dqt == ND - 1:
                            ins.then_inc(t.h, 1)
            P.t(womm)
            wotv = t.inc()

            def zr(e, k=k, j=j, dxw=dxw, lnC=lnC, ep=list(ep_v)):
                e.wait_ge(dxw[0], dxw[1])
                e.wait_ge(v.h, lnC)
                if ep:
                    e.wait_ge(v.h, ep[-1])
                e.activation(zres[:, :], xbuf[:, j * D:(j + 1) * D],
                             AF.Identity, bias=nbias[:, k:k + 1],
                             scale=rstd[:, k:k + 1]).then_inc(sc.h, 1)
            P.s(zr)
            zrv = sc.inc()

            def aadd(e, k=k, wotv=wotv, zrv=zrv, ep=list(ep_v)):
                e.wait_ge(t.h, wotv)
                e.wait_ge(sc.h, zrv)
                if ep:
                    e.wait_ge(v.h, ep[-1])
                e.tensor_tensor(trash[:, :], zres[:, :], ps_wo[:, :],
                                op=ALU.add).then_inc(v.h, 1)
            P.v(aadd)
            aav = v.inc()

            def est_v(e, aav=aav):
                e.wait_ge(v.h, aav)
                e.reduce_sum(s1[:, 0:1], trash[:, :], axis=AX.X).then_inc(v.h, 1)
            P.v(est_v)
            e1v = v.inc()

            def est_s(e, aav=aav):
                e.wait_ge(v.h, aav)
                e.activation(zres[:, :], trash[:, :], AF.Square,
                             accum_out=s2[:, 0:1]).then_inc(sc.h, 1)
            P.s(est_s)
            e1s = sc.inc()

            lnE = emit_ln_finalize(P, S, s1, s2, rstd, nbias, tmp, D, EPS, 1,
                                   wait=[(v.h, e1v), (sc.h, e1s)])

            def nrmE(e, k=k, lnE=lnE):
                e.wait_ge(v.h, lnE)
                e.activation(zres[:, :], trash[:, :], AF.Identity,
                             bias=nbias[:, 0:1],
                             scale=rstd[:, 0:1]).then_inc(sc.h, 1)
            P.s(nrmE)
            nev = sc.inc()

            def addE(e, k=k, j=j, nev=nev):
                e.wait_ge(sc.h, nev)
                e.tensor_tensor(xbuf[:, j * D:(j + 1) * D],
                                xbuf[:, j * D:(j + 1) * D], zres[:, :],
                                op=ALU.add).then_inc(v.h, 1)
            P.v(addE)
            ep_v.append(v.inc())

            def stx(e, k=k, j=j, av=ep_v[-1], sh=qst[j].h):
                e.wait_ge(v.h, av)
                e.dma_start(out=x3_out[k * 128:(k + 1) * 128, :],
                            in_=xbuf[:, j * D:(j + 1) * D]).then_inc(sh, 16)
            P.y(stx)
            st3_q.append(qst[j].inc(16))

        def finC(e, l0=qst[0].n, l1=qst[1].n):
            e.wait_ge(qst[0].h, l0)
            e.wait_ge(qst[1].h, l1)
        P.y(finC)


def build_c_program():
    nc = bass.Bass("TRN2", target_bir_lowering=False)
    x2_in = nc.dram_tensor("x2", [T, D], F32, kind="ExternalInput").ap()
    wqt = nc.dram_tensor("wqt", [D, D], F32, kind="ExternalInput").ap()
    wkt = nc.dram_tensor("wkt", [D, 128], F32, kind="ExternalInput").ap()
    wvt = nc.dram_tensor("wvt", [D, 128], F32, kind="ExternalInput").ap()
    wot = nc.dram_tensor("wot", [D, D], F32, kind="ExternalInput").ap()
    nsc = nc.dram_tensor("nscol", [128, ND], F32, kind="ExternalInput").ap()
    ssc = nc.dram_tensor("sscol", [128, ND], F32, kind="ExternalInput").ap()
    ident_in = nc.dram_tensor("ident", [128, 128], BF16,
                              kind="ExternalInput").ap()
    x3_out = nc.dram_tensor("x3", [T, D], F32, kind="ExternalOutput").ap()
    with contextlib.ExitStack() as ctx:
        S = mk_sems(nc, ctx)
        P = Prog(nc)
        build_phase_c(nc, ctx, P, S, x2_in, wqt, wkt, wvt, wot, nsc, ssc,
                      x3_out, ident_in)
        P.emit()
    return nc



NI = 44
INTER = 5632


def build_phase_d(nc, ctx, P, S, x3_in, gate_t, up_t, down_t, gs_in, us_in,
                  ds_in, x4_out, ident_in, gate_b, up_b, down_b,
                  wcast_done=None):
    t, v, sc = S['t'], S['v'], S['s']
    qmi, gms, gmw = S['qmi'], S['gms'], S['gmw']
    qx = [S['qx0'], S['qx1']]
    qst = [S['qst0'], S['qst1']]
    gwg = [S['gwg0'], S['gwg1'], S['gwg2'], S['gwg3']]
    gwd = [S['gwd0'], S['gwd1'], S['gwd2'], S['gwd3']]

    id_sb = ctx.enter_context(nc.sbuf_tensor("id_sbD", [128, 128], BF16))
    xbuf = ctx.enter_context(nc.sbuf_tensor("xbufD", [128, 2 * D], F32))
    trash = ctx.enter_context(nc.sbuf_tensor("trashD", [128, D], F32))
    s1 = ctx.enter_context(nc.sbuf_tensor("s1D", [128, NT], F32))
    s2 = ctx.enter_context(nc.sbuf_tensor("s2D", [128, NT], F32))
    rstd = ctx.enter_context(nc.sbuf_tensor("rstdD", [128, NT], F32))
    nbias = ctx.enter_context(nc.sbuf_tensor("nbiasD", [128, NT], F32))
    tmp = ctx.enter_context(nc.sbuf_tensor("tmpD", [128, 3 * NT], F32))
    s1b = ctx.enter_context(nc.sbuf_tensor("s1bD", [128, 4], F32))
    s2b = ctx.enter_context(nc.sbuf_tensor("s2bD", [128, 4], F32))
    rstdb = ctx.enter_context(nc.sbuf_tensor("rstdbD", [128, 4], F32))
    nbiasb = ctx.enter_context(nc.sbuf_tensor("nbiasbD", [128, 4], F32))
    tmpb = ctx.enter_context(nc.sbuf_tensor("tmpbD", [128, 12], F32))
    hd = ctx.enter_context(nc.sbuf_tensor("hdD", [128, NI * 512], BF16))
    zt = ctx.enter_context(nc.sbuf_tensor("ztD", [128, ND * 512], BF16))
    zstage = ctx.enter_context(nc.sbuf_tensor("zstageD", [128, 2 * D], BF16))
    wg = ctx.enter_context(nc.sbuf_tensor("wgD", [128, 4 * D], BF16))
    wu = ctx.enter_context(nc.sbuf_tensor("wuD", [128, 4 * D], BF16))
    wd = ctx.enter_context(nc.sbuf_tensor("wdD", [128, 4 * 1024], BF16))
    och = ctx.enter_context(nc.sbuf_tensor("ochD", [128, 4 * D], F32))
    sgb = ctx.enter_context(nc.sbuf_tensor("sgbD", [128, 512], BF16))
    gsc = ctx.enter_context(nc.sbuf_tensor("gscD", [128, NI], F32))
    usc = ctx.enter_context(nc.sbuf_tensor("uscD", [128, NI], F32))
    dsrow = ctx.enter_context(nc.sbuf_tensor("dsrowD", [1, D], F32))
    onesD = ctx.enter_context(nc.sbuf_tensor("onesD", [1, 128], F32))
    dsrep = ctx.enter_context(nc.sbuf_tensor("dsrepD", [128, D], F32))

    ps_tp = ctx.enter_context(nc.psum_tensor("ps_tpD", [128, 1024], BF16))
    ps_g = ctx.enter_context(nc.psum_tensor("ps_gD", [128, 512], F32))
    ps_u = ctx.enter_context(nc.psum_tensor("ps_uD", [128, 512], F32))
    ps_o = [ctx.enter_context(nc.psum_tensor(f"ps_oD{i}", [128, 512], F32))
            for i in range(4)]
    ps_sp = ctx.enter_context(nc.psum_tensor("ps_spD", [128, 512], F32))

    if wcast_done is None:
        def wcast(e):
            e.dma_start(out=gate_b, in_=gate_t).then_inc(gmw.h, 16)
            e.dma_start(out=up_b, in_=up_t).then_inc(gmw.h, 16)
            e.dma_start(out=down_b, in_=down_t).then_inc(gmw.h, 16)
        P.g(wcast)
        wb_g = gmw.inc(48)
    else:
        wb_g = wcast_done

    def ld0(e):
        e.dma_start(out=id_sb[:, :], in_=ident_in).then_inc(qmi.h, 16)
        e.dma_start(out=gsc[:, :], in_=gs_in).then_inc(qmi.h, 16)
        e.dma_start(out=usc[:, :], in_=us_in).then_inc(qmi.h, 16)
        e.dma_start(out=dsrow[0:1, :], in_=ds_in[None, :]).then_inc(qmi.h, 16)
    P.y(ld0)
    c0_q = qmi.inc(64)

    def msetD(e):
        e.memset(onesD[:, :], 1.0)
        e.dma_start(out=onesD[0:1, 0:8], in_=onesD[0:1, 0:8]).then_inc(gms.h, 16)
    P.g(msetD)
    ones_g = gms.inc(16)

    dse_v = []
    for et in range(4):
        def dbc(e, et=et, c0=c0_q, og=ones_g, dv=list(dse_v)):
            if et == 0:
                e.wait_ge(qmi.h, c0)
                e.wait_ge(gms.h, og)
            else:
                e.wait_ge(v.h, dv[-1])
            e.matmul(ps_sp[:, :], onesD[0:1, :],
                     dsrow[0:1, et * 512:(et + 1) * 512],
                     start=True, stop=True).then_inc(t.h, 1)
        P.t(dbc)
        dtv = t.inc()

        def dbe(e, et=et, dtv=dtv):
            e.wait_ge(t.h, dtv)
            e.tensor_copy(dsrep[:, et * 512:(et + 1) * 512],
                          ps_sp[:, :]).then_inc(v.h, 1)
        P.v(dbe)
        dse_v.append(v.inc())
    ds_ready = dse_v[-1]

    statD_v, statD_s = [], []
    for k in range(NT):
        j = k % 2

        def ld(e, k=k, j=j, sv=list(statD_v), ss=list(statD_s), qh=qx[j].h):
            if k >= 2:
                e.wait_ge(v.h, sv[k - 2])
                e.wait_ge(sc.h, ss[k - 2])
            e.dma_start(out=xbuf[:, j * D:(j + 1) * D],
                        in_=x3_in[k * 128:(k + 1) * 128, :]).then_inc(qh, 16)
        P.y(ld)
        dw = (qx[j].h, qx[j].inc(16))
        vv, sv_ = emit_stats_tile(P, S, xbuf[:, j * D:(j + 1) * D], k,
                                  s1, s2, trash[:, :], dw)
        statD_v.append(vv)
        statD_s.append(sv_)

    lnD = emit_ln_finalize(P, S, s1, s2, rstd, nbias, tmp, D, EPS, NT,
                           wait=[(v.h, statD_v[-1]), (sc.h, statD_s[-1])])

    oev_all = []
    guev = []
    omm_t = []
    st4_q = []
    last_add_v = [None]
    last_zh_s = [statD_s[-1]]
    for tc in range(4):
        tpD_v = []
        zh_chunk = []
        for ti in range(4):
            k = tc * 4 + ti
            j = ti % 2

            def ldx(e, k=k, j=j, ti=ti, tc=tc, sq=list(st4_q),
                    zs0=last_zh_s[0], zv0=statD_v[-1], zc=list(zh_chunk),
                    qh=qx[j].h, sh=qst[j].h):
                if ti >= 2:
                    e.wait_ge(sc.h, zc[ti - 2])
                elif tc == 0:
                    e.wait_ge(sc.h, zs0)
                    e.wait_ge(v.h, zv0)
                elif len(sq) >= 2:
                    e.wait_ge(sh, sq[-2])
                e.dma_start(out=xbuf[:, j * D:(j + 1) * D],
                            in_=x3_in[k * 128:(k + 1) * 128, :]
                            ).then_inc(qh, 16)
            P.y(ldx)
            dxw = (qx[j].h, qx[j].inc(16))

            def zh(e, k=k, j=j, ti=ti, dxw=dxw, lnD=lnD, tv=list(tpD_v),
                   ov=list(oev_all)):
                e.wait_ge(dxw[0], dxw[1])
                e.wait_ge(v.h, lnD)
                if ti >= 2:
                    e.wait_ge(v.h, tv[-3])
                elif ov:
                    e.wait_ge(v.h, ov[-1])
                e.activation(zstage[:, j * D:(j + 1) * D],
                             xbuf[:, j * D:(j + 1) * D],
                             AF.Identity, bias=nbias[:, k:k + 1],
                             scale=rstd[:, k:k + 1]).then_inc(sc.h, 1)
            P.s(zh)
            zh_chunk.append(sc.inc())

            for grp in range(2):
                def tp(e, j=j, ti=ti, grp=grp, zs=zh_chunk[-1], c0=c0_q,
                       tv=list(tpD_v)):
                    e.wait_ge(sc.h, zs)
                    if ti == 0 and grp == 0:
                        e.wait_ge(qmi.h, c0)
                    if len(tv) >= 1:
                        e.wait_ge(v.h, tv[-1])
                    for jj in range(8):
                        dt = grp * 8 + jj
                        ins = e.transpose(
                            ps_tp[:, jj * 128:(jj + 1) * 128],
                            zstage[:, j * D + dt * 128:j * D + (dt + 1) * 128],
                            id_sb[:, :])
                        if jj == 7:
                            ins.then_inc(t.h, 1)
                P.t(tp)
                tval = t.inc()

                def ev(e, ti=ti, grp=grp, tval=tval):
                    e.wait_ge(t.h, tval)
                    for jj in range(8):
                        dt = grp * 8 + jj
                        ins = e.tensor_copy(
                            zt[:, dt * 512 + ti * 128:dt * 512 + (ti + 1) * 128],
                            ps_tp[:, jj * 128:(jj + 1) * 128])
                        if jj == 7:
                            ins.then_inc(v.h, 1)
                P.v(ev)
                tpD_v.append(v.inc())
        zt_v = tpD_v[-1]
        last_zh_s[0] = zh_chunk[-1]

        hd_chunk = []
        sil_chunk = []
        for it in range(NI):
            par = it % 4

            def ldw(e, it=it, par=par, wbg=wb_g, gv=list(guev),
                    gh=gwg[par].h):
                if it < 4 and not gv:
                    e.wait_ge(gmw.h, wbg)
                if len(gv) >= 4:
                    e.wait_ge(t.h, gv[-4])
                e.dma_start(out=wg[:, par * D:(par + 1) * D],
                            in_=gate_b[it]).then_inc(gh, 16)
                e.dma_start(out=wu[:, par * D:(par + 1) * D],
                            in_=up_b[it]).then_inc(gh, 16)
            P.g(ldw)
            ww = (gwg[par].h, gwg[par].inc(32))

            def gmmG(e, it=it, par=par, ww=ww, ztv=zt_v, sl=list(sil_chunk),
                     oa=list(oev_all)):
                e.wait_ge(ww[0], ww[1])
                e.wait_ge(v.h, ztv)
                if sl:
                    e.wait_ge(sc.h, sl[-1])
                for dt in range(ND):
                    ins = e.matmul(
                        ps_g[:, :],
                        wg[:, par * D + dt * 128:par * D + (dt + 1) * 128],
                        zt[:, dt * 512:(dt + 1) * 512],
                        start=(dt == 0), stop=(dt == ND - 1))
                    if dt == ND - 1:
                        ins.then_inc(t.h, 1)
            P.t(gmmG)
            gvG = t.inc()

            def gmmU(e, it=it, par=par, ww=ww, ztv=zt_v, hv=list(hd_chunk),
                     oa=list(oev_all)):
                if hv:
                    e.wait_ge(v.h, hv[-1])
                elif oa:
                    e.wait_ge(v.h, oa[-1])
                for dt in range(ND):
                    ins = e.matmul(
                        ps_u[:, :],
                        wu[:, par * D + dt * 128:par * D + (dt + 1) * 128],
                        zt[:, dt * 512:(dt + 1) * 512],
                        start=(dt == 0), stop=(dt == ND - 1))
                    if dt == ND - 1:
                        ins.then_inc(t.h, 1)
            P.t(gmmU)
            guev.append(t.inc())

            def sil(e, it=it, gv=gvG):
                e.wait_ge(t.h, gv)
                e.activation(sgb[:, :], ps_g[:, :], AF.Silu,
                             bias=0.0, scale=gsc[:, it:it + 1]).then_inc(sc.h, 1)
            P.s(sil)
            silv = sc.inc()
            sil_chunk.append(silv)

            def hdw(e, it=it, silv=silv, uv=guev[-1]):
                e.wait_ge(sc.h, silv)
                e.wait_ge(t.h, uv)
                e.scalar_tensor_tensor(hd[:, it * 512:(it + 1) * 512],
                                       ps_u[:, :], usc[:, it:it + 1],
                                       sgb[:, :], op0=ALU.mult,
                                       op1=ALU.mult).then_inc(v.h, 1)
            P.v(hdw)
            hd_chunk.append(v.inc())
        hd_ready = hd_chunk[-1]

        oev_chunk = []
        for p_ in range(2):
            for hh in range(2):
                for it in range(NI):
                    par = it % 4

                    def ldd(e, it=it, par=par, hh=hh, wbg=wb_g,
                            ot=list(omm_t), gh=gwd[par].h):
                        if not ot and it < 4:
                            e.wait_ge(gmw.h, wbg)
                        if len(ot) >= 4:
                            e.wait_ge(t.h, ot[-4])
                        e.dma_start(
                            out=wd[:, par * 1024:(par + 1) * 1024],
                            in_=down_b[it * 128:(it + 1) * 128,
                                       hh * 1024:(hh + 1) * 1024]
                            ).then_inc(gh, 16)
                    P.g(ldd)
                    dww = (gwd[par].h, gwd[par].inc(16))

                    def omm(e, it=it, par=par, p_=p_, hh=hh, dww=dww,
                            hr=hd_ready, dsr=ds_ready, oc=list(oev_chunk),
                            oa=list(oev_all), la=last_add_v[0]):
                        e.wait_ge(dww[0], dww[1])
                        if it == 0:
                            e.wait_ge(v.h, hr)
                            e.wait_ge(v.h, dsr)
                            if oc:
                                e.wait_ge(v.h, oc[-1])
                            elif oa:
                                e.wait_ge(v.h, oa[-1])
                        for sub in range(4):
                            tt = 2 * p_ + sub // 2
                            hc = 2 * hh + sub % 2
                            ins = e.matmul(
                                ps_o[sub][:, :],
                                hd[:, it * 512 + tt * 128:it * 512 + tt * 128 + 128],
                                wd[:, par * 1024 + (hc % 2) * 512:
                                   par * 1024 + (hc % 2) * 512 + 512],
                                start=(it == 0), stop=(it == NI - 1))
                            if sub == 3:
                                ins.then_inc(t.h, 1)
                    P.t(omm)
                    omm_t.append(t.inc())

                def oev(e, p_=p_, hh=hh, otv=omm_t[-1], dsr=ds_ready,
                        la=last_add_v[0]):
                    e.wait_ge(t.h, otv)
                    if la is not None:
                        e.wait_ge(v.h, la)
                    for sub in range(4):
                        tt = 2 * p_ + sub // 2
                        hc = 2 * hh + sub % 2
                        ins = e.tensor_tensor(
                            och[:, tt * D + hc * 512:tt * D + (hc + 1) * 512],
                            ps_o[sub][:, :],
                            dsrep[:, hc * 512:(hc + 1) * 512], op=ALU.mult)
                        if sub == 3:
                            ins.then_inc(v.h, 1)
                P.v(oev)
                oev_chunk.append(v.inc())
                oev_all.append(oev_chunk[-1])

        cst_v, cst_s = [], []
        for ti in range(4):
            def ost_v(e, ti=ti, ov=oev_chunk[-1]):
                e.wait_ge(v.h, ov)
                e.reduce_sum(s1b[:, ti:ti + 1], och[:, ti * D:(ti + 1) * D],
                             axis=AX.X).then_inc(v.h, 1)
            P.v(ost_v)
            cst_v.append(v.inc())

            def ost_s(e, ti=ti, ov=oev_chunk[-1]):
                e.wait_ge(v.h, ov)
                e.activation(trash[:, :], och[:, ti * D:(ti + 1) * D],
                             AF.Square, accum_out=s2b[:, ti:ti + 1]
                             ).then_inc(sc.h, 1)
            P.s(ost_s)
            cst_s.append(sc.inc())

        lnO = emit_ln_finalize(P, S, s1b, s2b, rstdb, nbiasb, tmpb, D, EPS, 4,
                               wait=[(v.h, cst_v[-1]), (sc.h, cst_s[-1])])

        addD_v = []
        for ti in range(4):
            k = tc * 4 + ti
            j = ti % 2

            def ldx2(e, k=k, j=j, lnO=lnO, sq=list(st4_q), qh=qx[j].h,
                     sh=qst[j].h):
                e.wait_ge(v.h, lnO)
                if len(sq) >= 2:
                    e.wait_ge(sh, sq[-2])
                e.dma_start(out=xbuf[:, j * D:(j + 1) * D],
                            in_=x3_in[k * 128:(k + 1) * 128, :]
                            ).then_inc(qh, 16)
            P.y(ldx2)
            dxw = (qx[j].h, qx[j].inc(16))

            def nrmO(e, ti=ti, lnO=lnO, av=list(addD_v)):
                e.wait_ge(v.h, lnO)
                if av:
                    e.wait_ge(v.h, av[-1])
                e.activation(trash[:, :], och[:, ti * D:(ti + 1) * D],
                             AF.Identity, bias=nbiasb[:, ti:ti + 1],
                             scale=rstdb[:, ti:ti + 1]).then_inc(sc.h, 1)
            P.s(nrmO)
            nov = sc.inc()

            def addD(e, j=j, nov=nov, dxw=dxw):
                e.wait_ge(sc.h, nov)
                e.wait_ge(dxw[0], dxw[1])
                e.tensor_tensor(xbuf[:, j * D:(j + 1) * D],
                                xbuf[:, j * D:(j + 1) * D], trash[:, :],
                                op=ALU.add).then_inc(v.h, 1)
            P.v(addD)
            addD_v.append(v.inc())

            def stx(e, k=k, j=j, av=addD_v[-1], sh=qst[j].h):
                e.wait_ge(v.h, av)
                e.dma_start(out=x4_out[k * 128:(k + 1) * 128, :],
                            in_=xbuf[:, j * D:(j + 1) * D]).then_inc(sh, 16)
            P.y(stx)
            st4_q.append(qst[j].inc(16))
        last_add_v[0] = addD_v[-1]

    def finD(e, l0=qst[0].n, l1=qst[1].n):
        e.wait_ge(qst[0].h, l0)
        e.wait_ge(qst[1].h, l1)
    P.y(finD)


def build_d_program():
    nc = bass.Bass("TRN2", target_bir_lowering=False)
    x3_in = nc.dram_tensor("x3", [T, D], F32, kind="ExternalInput").ap()
    gate_t = nc.dram_tensor("gate_t", [NI, 128, D], F32,
                            kind="ExternalInput").ap()
    up_t = nc.dram_tensor("up_t", [NI, 128, D], F32, kind="ExternalInput").ap()
    down_t = nc.dram_tensor("down_t", [INTER, D], F32,
                            kind="ExternalInput").ap()
    gs_in = nc.dram_tensor("gs", [128, NI], F32, kind="ExternalInput").ap()
    us_in = nc.dram_tensor("us", [128, NI], F32, kind="ExternalInput").ap()
    ds_in = nc.dram_tensor("ds", [D], F32, kind="ExternalInput").ap()
    ident_in = nc.dram_tensor("ident", [128, 128], BF16,
                              kind="ExternalInput").ap()
    x4_out = nc.dram_tensor("x4", [T, D], F32, kind="ExternalOutput").ap()
    gate_b = nc.dram_tensor("gate_b", [NI, 128, D], BF16, kind="Internal").ap()
    up_b = nc.dram_tensor("up_b", [NI, 128, D], BF16, kind="Internal").ap()
    down_b = nc.dram_tensor("down_b", [INTER, D], BF16, kind="Internal").ap()
    with contextlib.ExitStack() as ctx:
        S = mk_sems(nc, ctx)
        P = Prog(nc)
        build_phase_d(nc, ctx, P, S, x3_in, gate_t, up_t, down_t, gs_in,
                      us_in, ds_in, x4_out, ident_in, gate_b, up_b, down_b)
        P.emit()
    return nc



def emit_barrier(P, S):
    cur = {k: S[k].n for k in S}

    def mk(me):
        def go(e, cur=dict(cur)):
            for k, ct in S.items():
                if cur[k] > 0:
                    e.wait_ge(ct.h, cur[k])
        return go
    P.t(mk('t'))
    P.v(mk('v'))
    P.s(mk('s'))
    P.g(mk('g'))
    P.y(mk('q'))


def build_full_program():
    nc = bass.Bass("TRN2", target_bir_lowering=False)
    x_in = nc.dram_tensor("x", [T, D], F32, kind="ExternalInput").ap()
    facs = nc.dram_tensor("factors", [8, D, D], F32, kind="ExternalInput").ap()
    fbias = nc.dram_tensor("fbias", [D], F32, kind="ExternalInput").ap()
    wqt = nc.dram_tensor("wqt", [D, D], F32, kind="ExternalInput").ap()
    wkt = nc.dram_tensor("wkt", [D, 128], F32, kind="ExternalInput").ap()
    wvt = nc.dram_tensor("wvt", [D, 128], F32, kind="ExternalInput").ap()
    wot = nc.dram_tensor("wot", [D, D], F32, kind="ExternalInput").ap()
    nsc = nc.dram_tensor("nscol", [128, ND], F32, kind="ExternalInput").ap()
    ssc = nc.dram_tensor("sscol", [128, ND], F32, kind="ExternalInput").ap()
    gate_t = nc.dram_tensor("gate_t", [NI, 128, D], F32,
                            kind="ExternalInput").ap()
    up_t = nc.dram_tensor("up_t", [NI, 128, D], F32, kind="ExternalInput").ap()
    down_t = nc.dram_tensor("down_t", [INTER, D], F32,
                            kind="ExternalInput").ap()
    gs_in = nc.dram_tensor("gs", [128, NI], F32, kind="ExternalInput").ap()
    us_in = nc.dram_tensor("us", [128, NI], F32, kind="ExternalInput").ap()
    ds_in = nc.dram_tensor("ds", [D], F32, kind="ExternalInput").ap()
    ident_in = nc.dram_tensor("ident", [128, 128], BF16,
                              kind="ExternalInput").ap()
    x4_out = nc.dram_tensor("x4", [T, D], F32, kind="ExternalOutput").ap()

    c_scr = nc.dram_tensor("c_scr", [T, D], BF16, kind="Internal").ap()
    fscr = nc.dram_tensor("fscr", [D, D], BF16, kind="Internal").ap()
    x1_scr = nc.dram_tensor("x1_scr", [T, D], F32, kind="Internal").ap()
    x2_scr = nc.dram_tensor("x2_scr", [T, D], F32, kind="Internal").ap()
    x3_scr = nc.dram_tensor("x3_scr", [T, D], F32, kind="Internal").ap()
    gate_b = nc.dram_tensor("gate_b", [NI, 128, D], BF16, kind="Internal").ap()
    up_b = nc.dram_tensor("up_b", [NI, 128, D], BF16, kind="Internal").ap()
    down_b = nc.dram_tensor("down_b", [INTER, D], BF16, kind="Internal").ap()

    with contextlib.ExitStack() as sems_ctx:
        S = mk_sems(nc, sems_ctx)
        P = Prog(nc)
        gmw = S['gmw']

        with contextlib.ExitStack() as ctx:
            fbuf = ctx.enter_context(nc.sbuf_tensor("fbuf", [128, 3 * D],
                                                    F32))
            fstate = {}

            def factor_cb(k):
                emit_factor_group(P, S, facs, fbuf, fscr, k, fstate)

            build_phase_a(nc, ctx, P, S, x_in, x1_scr, ident_in, c_scr,
                          factor_cb=factor_cb)
            emit_barrier(P, S)
        fsum_wait = (S['gfc0'].h, S['gfc0'].n)
        with contextlib.ExitStack() as ctx:
            build_phase_b(nc, ctx, P, S, x1_scr, facs, fbias, x2_scr, ident_in,
                          fsum_pre=(fscr, fsum_wait))
            def wcast(e):
                e.dma_start(out=gate_b, in_=gate_t).then_inc(gmw.h, 16)
                e.dma_start(out=up_b, in_=up_t).then_inc(gmw.h, 16)
                e.dma_start(out=down_b, in_=down_t).then_inc(gmw.h, 16)
            P.g(wcast)
            wb_g = gmw.inc(48)
            emit_barrier(P, S)
        with contextlib.ExitStack() as ctx:
            build_phase_c(nc, ctx, P, S, x2_scr, wqt, wkt, wvt, wot, nsc, ssc,
                          x3_scr, ident_in)
            emit_barrier(P, S)
        with contextlib.ExitStack() as ctx:
            build_phase_d(nc, ctx, P, S, x3_scr, gate_t, up_t, down_t, gs_in,
                          us_in, ds_in, x4_out, ident_in, gate_b, up_b,
                          down_b, wcast_done=wb_g)
            P.emit()
    return nc
